# version 19
# baseline (speedup 1.0000x reference)
import numpy as np
import ml_dtypes
import concourse.bass as bass
import concourse.mybir as mybir
from concourse.bass_utils import run_bass_kernel_spmd

F32 = mybir.dt.float32
BF16 = mybir.dt.bfloat16
AF = mybir.ActivationFunctionType
ALU = mybir.AluOpType
I32 = mybir.dt.int32
MAGICF = float(0x5f3759df)

L = 4096
D = 1024
T = 256
NT = L // T
H = 4
DK = 256
DV = 512
EPS = np.float32(1e-5)
LN16 = np.float32(np.log(16.0))
S116 = np.float32(1.0 / 16.0)

ENGS = ("tensor", "vector", "scalar", "gpsimd", "sync", "pool")


class Buf:
    __slots__ = ("name", "ready", "users")

    def __init__(self, name):
        self.name = name
        self.ready = None
        self.users = []


class Sch:

    def __init__(self, nc, paranoid=False):
        self.nc = nc
        self.paranoid = paranoid
        self.q = {e: [] for e in ENGS}
        self.cnt = {e: 0 for e in ENGS}
        self.sem = {}
        self.lastwait = {e: {} for e in ENGS}

    def need(self, eng, tok):
        if tok is None:
            return
        peng, val = tok
        if peng == eng:
            if not (self.paranoid and eng in ("vector", "scalar", "pool")):
                return
        if peng == "pool" and eng == "gpsimd":
            return
        if self.lastwait[eng].get(peng, 0) >= val:
            return
        self.lastwait[eng][peng] = val
        self.q["gpsimd" if eng == "pool" else eng].append(
            lambda E, p=peng, v=val: E.wait_ge(self.sem[p], v))

    def group(self, eng, fns, reads=(), writes=(), dma=False):
        for b in reads:
            self.need(eng, b.ready)
        for b in writes:
            self.need(eng, b.ready)
            for u in b.users:
                self.need(eng, u)
        phys = "gpsimd" if eng == "pool" else eng
        n = 16 if dma else 1
        if dma and self.cnt[eng]:
            self.q[phys].append(
                lambda E, e=eng, v=self.cnt[eng]: E.wait_ge(self.sem[e], v))
        for fn in fns[:-1]:
            self.q[phys].append(fn)
        self.cnt[eng] += n
        val = self.cnt[eng]

        def last(E, fn=fns[-1], e=eng, k=n):
            fn(E).then_inc(self.sem[e], k)

        self.q[phys].append(last)
        tok = (eng, val)
        for b in reads:
            b.users.append(tok)
        for b in writes:
            b.ready = tok
            b.users = []
        return tok

    def raw_wait(self, eng, peng, val):
        self.q[eng].append(lambda E, p=peng, v=val: E.wait_ge(self.sem[p], v))


def build_program(phases=("A1", "A2", "B"), paranoid=False):
    nc = bass.Bass("TRN2", target_bir_lowering=False)
    dt = nc.dram_tensor
    x_d = dt("x", [L, D], F32, kind="ExternalInput")
    xt_d = dt("xt", [D, L], F32, kind="ExternalInput")
    wq_d = dt("wq", [D, 1024], F32, kind="ExternalInput")
    wk_d = dt("wk", [D, 1024], F32, kind="ExternalInput")
    wv_d = dt("wv", [D, 2048], F32, kind="ExternalInput")
    wg_d = dt("wg", [D, 2048], F32, kind="ExternalInput")
    gw1_d = dt("gw1", [D, 16], F32, kind="ExternalInput")
    gw2_d = dt("gw2", [16, D], F32, kind="ExternalInput")
    wo_d = dt("wo", [2048, D], F32, kind="ExternalInput")
    wrt_d = dt("wrt", [D, D], F32, kind="ExternalInput")
    wkt_d = dt("wkt", [D, D], F32, kind="ExternalInput")
    wvt_d = dt("wvt", [D, D], F32, kind="ExternalInput")
    ident_d = dt("ident", [128, 128], F32, kind="ExternalInput")
    mask_d = dt("maskut", [128, 128], F32, kind="ExternalInput")
    onesb_d = dt("onesb", [128, 8], F32, kind="ExternalInput")
    onesr_d = dt("onesr", [1, 128], F32, kind="ExternalInput")
    b2c_d = dt("b2c", [128, 8], F32, kind="ExternalInput")
    gnw4_d = dt("gnw4", [128, 4], F32, kind="ExternalInput")
    murc_d = dt("murc", [128, 8], F32, kind="ExternalInput")
    mukc_d = dt("mukc", [128, 8], F32, kind="ExternalInput")

    wgbf_d = dt("wgbf", [D, 2048], BF16)
    wobf_d = dt("wobf", [2048, D], BF16)
    wrtbf_d = dt("wrtbf", [D, D], BF16)
    wktbf_d = dt("wktbf", [D, D], BF16)
    wvtbf_d = dt("wvtbf", [D, D], BF16)
    if "A1" in phases:
        orms_d = dt("orms", [128, 16, L], BF16, **({} if "A2" in phases else {"kind": "ExternalOutput"}))
    else:
        orms_d = dt("orms", [128, 16, L], BF16, kind="ExternalInput")
    if "A2" in phases:
        ybuf_d = dt("ybuf", [L, D], F32, **({} if "B" in phases else {"kind": "ExternalOutput"}))
    elif "B" in phases:
        ybuf_d = dt("ybuf", [L, D], F32, kind="ExternalInput")
    if "B" in phases:
        out_d = dt("out", [L, D], F32, kind="ExternalOutput")

    AR_COLS = 32896

    from contextlib import ExitStack
    with ExitStack() as ES:
        def sb(name, shape, dt_):
            return ES.enter_context(nc.sbuf_tensor(name, shape, dt_))

        def ps(name, shape):
            return ES.enter_context(nc.psum_tensor(name, shape, F32))

        arena = sb("arena", [128, AR_COLS], BF16)
        gw2s = sb("gw2s", [16, 1024], BF16)
        z1b = sb("z1b", [16, 256], BF16)
        ident = sb("ident_s", [128, 128], BF16)
        maskut = sb("maskut_s", [128, 128], BF16)
        onesb = sb("onesb_s", [128, 8], BF16)
        onesr = sb("onesr_s", [1, 128], F32)
        b2c = sb("b2c_s", [128, 8], F32)
        gnw4 = sb("gnw4_s", [128, 4], F32)
        murc = sb("murc_s", [128, 8], F32)
        mukc = sb("mukc_s", [128, 8], F32)
        b2n = sb("b2n_s", [128, 8], F32)
        eac = sb("eac", [128, 8], F32)
        rows = sb("rows", [1, 512], F32)
        bcs = sb("bcs", [128, 256], F32)
        statr = sb("statr", [128, 32], F32)
        rowsA = sb("rowsA", [128, 128], F32)
        rstdA = sb("rstdA", [128, 128], F32)
        pcol0 = sb("pcol0", [128, 8], BF16)
        pcol1 = sb("pcol1", [128, 8], BF16)
        F1t = sb("F1", [128, 2048], F32)
        F2t = sb("F2", [128, 2048], F32)
        B1t = sb("B1", [128, 2048], BF16)
        B1u = sb("B1u", [128, 2048], BF16)
        B4u = sb("B4u", [128, 2048], BF16)
        B5u = sb("B5u", [128, 2048], BF16)
        SQ1 = sb("SQ1", [128, 2048], BF16)
        B2t = sb("B2", [128, 2048], BF16)
        B3t = sb("B3", [128, 2048], BF16)
        B4t = sb("B4", [128, 2048], BF16)
        B5t = sb("B5", [128, 2048], BF16)
        B6t = sb("B6", [128, 2048], BF16)
        V1t = sb("V1", [128, 4096], BF16)
        PMt = sb("PM", [128, 384], BF16)
        O0 = sb("O0", [128, 4096], BF16)
        O1 = sb("O1", [128, 4096], BF16)
        X0 = sb("X0", [128, 2048], F32)
        X1 = sb("X1", [128, 2048], F32)
        Y0 = sb("Y0", [128, 2048], F32)
        Y1 = sb("Y1", [128, 2048], F32)
        Ss = sb("Ss", [128, 4096], F32)
        XT0 = X0[:, 0:1024].bitcast(BF16)
        XT1 = X1[:, 0:1024].bitcast(BF16)
        PBIG = ps("PBIG", [128, 2048])
        PO1 = ps("PO1", [128, 1024])
        PO2 = ps("PO2", [128, 1024])
        s_te = ES.enter_context(nc.semaphore("s_te"))
        s_ve = ES.enter_context(nc.semaphore("s_ve"))
        s_se = ES.enter_context(nc.semaphore("s_se"))
        s_gp = ES.enter_context(nc.semaphore("s_gp"))
        s_sy = ES.enter_context(nc.semaphore("s_sy"))
        s_pl = ES.enter_context(nc.semaphore("s_pl"))
        block = ES.enter_context(nc.Block())

        S = Sch(nc, paranoid=paranoid)
        sems = {"tensor": s_te, "vector": s_ve, "scalar": s_se,
                "gpsimd": s_gp, "sync": s_sy, "pool": s_pl}
        S.sem = sems

        bWBF = {}
        bARL = Buf("arenaL")
        bARH = Buf("arenaH")
        bGW2 = Buf("gw2")
        bZ1B = Buf("z1b")
        bCONST = Buf("const")
        bEAC = Buf("eac")
        bROWS = Buf("rows")
        bBCS = Buf("bcs")
        bSTAT = Buf("statr")
        bRA = Buf("rowsA")
        bRS = Buf("rstdA")
        bPCOL = [Buf("pcol0"), Buf("pcol1")]
        bF1 = Buf("F1"); bF2 = Buf("F2")
        bB1 = Buf("B1"); bB1u = Buf("B1u"); bSQ = Buf("SQ1")
        bB4u = Buf("B4u"); bB5u = Buf("B5u")
        sbf_bufs = [Buf("sbf0"), Buf("sbf1")]
        sq_bufs = [Buf("sq0"), Buf("sq1")]
        bB2 = Buf("B2"); bB3 = Buf("B3")
        bB4 = Buf("B4"); bB5 = Buf("B5"); bB6 = Buf("B6")
        bV1q = [Buf("V1q0"), Buf("V1q1"), Buf("V1q2"), Buf("V1q3")]
        bPMm = Buf("PM")
        bO = [Buf("O0"), Buf("O1")]
        bX = [Buf("X0"), Buf("X1")]
        bXT = bX
        bY = [Buf("Y0"), Buf("Y1")]
        bS = Buf("Ss")
        bPB = [Buf("PB0"), Buf("PB1"), Buf("PB2"), Buf("PB3")]
        bPO = [Buf("PO1"), Buf("PO2")]
        POt = [PO1, PO2]
        XTt = [XT0, XT1]
        Ot = [O0, O1]
        Xt = [X0, X1]
        Yt = [Y0, Y1]
        PCt = [pcol0, pcol1]

        A1_WQ, A1_WK, A1_WV, A1_GW1 = 0, 8192, 16384, 32768
        A2_WG, A2_WO = 0, 16384
        B_WR, B_WK, B_WV = 0, 8192, 16384

        def load_cast(dst_ap, src_ap):
            return lambda E, d=dst_ap, s=src_ap: E.dma_start(out=d, in_=s)

        S.group("gpsimd", [load_cast(ident[:, :], ident_d[:, :])], writes=[bCONST], dma=True)
        S.group("gpsimd", [load_cast(maskut[:, :], mask_d[:, :])], writes=[bCONST], dma=True)
        S.group("gpsimd", [load_cast(onesb[:, :], onesb_d[:, :])], writes=[bCONST], dma=True)
        S.group("sync", [load_cast(onesr[:, :], onesr_d[:, :])], writes=[bCONST], dma=True)
        S.group("sync", [load_cast(b2c[:, :], b2c_d[:, :])], writes=[bCONST], dma=True)
        S.group("sync", [load_cast(gnw4[:, :], gnw4_d[:, :])], writes=[bCONST], dma=True)
        S.group("sync", [load_cast(murc[:, :], murc_d[:, :])], writes=[bCONST], dma=True)
        S.group("sync", [load_cast(mukc[:, :], mukc_d[:, :])], writes=[bCONST], dma=True)
        S.group("vector", [lambda E: E.tensor_scalar(b2n[:, :], b2c[:, :], -1.0,
                                                     None, ALU.mult)],
                reads=[bCONST], writes=[bCONST])
        S.group("vector", [lambda E: E.memset(Ss[:, :], 0.0)], writes=[bS])
        S.group("vector", [lambda E: E.memset(rowsA[:, :], 1.0)], writes=[bRA])
        S.group("vector", [lambda E: E.memset(pcol0[:, :], 0.0)], writes=[bPCOL[0]])

        def arena_load(off, src, kc, ncols_per_kc):
            dst = arena[:, off:off + kc * ncols_per_kc].rearrange(
                "p (kc f) -> p kc f", kc=kc)
            src_ap = src[:, :].rearrange("(kc p) f -> p kc f", p=128)
            return lambda E, d=dst, s=src_ap: E.dma_start(out=d, in_=s)

        wconv_list = [("wg", wg_d, wgbf_d), ("wo", wo_d, wobf_d),
                      ("wrt", wrt_d, wrtbf_d), ("wkt", wkt_d, wktbf_d),
                      ("wvt", wvt_d, wvtbf_d)]

        def emit_wconv(j):
            name, src, dst = wconv_list[j]
            bWBF[name] = Buf("bf_" + name)
            S.group("gpsimd", [lambda E, s_=src, d_=dst: E.dma_start(
                out=d_[:, :], in_=s_[:, :])], writes=[bWBF[name]], dma=True)

        def load_phase_a1_weights():
            S.group("gpsimd", [arena_load(A1_WQ, wq_d, 8, 1024)], writes=[bARL], dma=True)
            S.group("gpsimd", [arena_load(A1_WK, wk_d, 8, 1024)], writes=[bARL], dma=True)
            S.group("gpsimd", [arena_load(A1_WV, wv_d, 8, 2048)], writes=[bARH], dma=True)
            S.group("gpsimd", [arena_load(A1_GW1, gw1_d, 8, 16)], writes=[bARH], dma=True)
            S.group("gpsimd", [load_cast(gw2s[:, :], gw2_d[:, :])], writes=[bGW2], dma=True)

        src_bufs = {}
        dst_bufs = {}

        def load_a1_x(i):
            t0 = i * T
            S.group("gpsimd",
                    [lambda E, d=[B1t, B1u][i % 2], t=t0: E.dma_start(
                        out=d[:, :].rearrange("p (kc t) -> p kc t", kc=8),
                        in_=xt_d[:, t:t + T].rearrange("(kc p) t -> p kc t", p=128))],
                    writes=[[bB1, bB1u][i % 2]], dma=True)

        def emit_a1_tile(i):
            t0 = i * T
            if i + 1 < NT:
                load_a1_x(i + 1)
            if "A2" in phases and i < len(wconv_list):
                emit_wconv(i)
            xfm = [B1t, B1u][i % 2]
            bXF = [bB1, bB1u][i % 2]
            fns = [lambda E, kc=kc, xfm=xfm: E.matmul(
                PO1[0:16, 0:256], arena[:, A1_GW1 + kc * 16:A1_GW1 + kc * 16 + 16],
                xfm[:, kc * 256:(kc + 1) * 256], start=(kc == 0), stop=(kc == 7))
                for kc in range(8)]
            S.group("tensor", fns, reads=[bARH, bXF], writes=[bPO[0]])
            S.group("vector", [lambda E: E.tensor_copy(z1b[:, :], PO1[0:16, 0:256])],
                    reads=[bPO[0]], writes=[bZ1B])
            fns = [lambda E, m=m: E.matmul(
                PBIG[:, m * 256:(m + 1) * 256], gw2s[0:16, m * 128:(m + 1) * 128],
                z1b[0:16, :], start=True, stop=True) for m in range(8)]
            S.group("tensor", fns, reads=[bGW2, bZ1B], writes=bPB)
            for bank in range(4):
                fns = [lambda E, kc=kc: E.activation(
                    F2t[:, kc * 256:(kc + 1) * 256], PBIG[:, kc * 256:(kc + 1) * 256],
                    AF.Exp, scale=np.float32(-1.0), bias=b2n[:, kc:kc + 1])
                    for kc in (2 * bank, 2 * bank + 1)]
                S.group("scalar", fns, reads=[bPB[bank], bCONST], writes=[bF2])
            def v_half(tb):
                for grp in range(4):
                    j = tb * 4 + grp
                    bank = j % 4
                    fns = [lambda E, kc=kc, tb=tb, grp=grp, bank=bank, xfm=xfm: E.matmul(
                        PBIG[:, bank * 512:(bank + 1) * 512],
                        xfm[:, kc * 256 + tb * 128:kc * 256 + tb * 128 + 128],
                        arena[:, A1_WV + kc * 2048 + grp * 512:A1_WV + kc * 2048 + grp * 512 + 512],
                        start=(kc == 0), stop=(kc == 7)) for kc in range(8)]
                    S.group("tensor", fns, reads=[bARH, bXF], writes=[bPB[bank]])
                    oap = V1t[:, tb * 2048 + grp * 512: tb * 2048 + grp * 512 + 512]
                    S.group("vector", [lambda E, o=oap, b=bank: E.tensor_copy(
                        o, PBIG[:, b * 512:(b + 1) * 512])],
                        reads=[bPB[bank]], writes=[bV1q[tb * 2 + grp // 2]])

            v_half(0)
            S.group("scalar", [lambda E: E.activation(F1t[:, :], F2t[:, :], AF.Ln,
                                                      bias=np.float32(1.0))],
                    reads=[bF2], writes=[bF1])
            fns = [lambda E, kc=kc: E.tensor_tensor_scan(
                F2t[:, kc * 256:(kc + 1) * 256], F1t[:, kc * 256:(kc + 1) * 256],
                F1t[:, kc * 256:(kc + 1) * 256], 0.0, ALU.add, ALU.bypass)
                for kc in range(8)]
            S.group("vector", fns, reads=[bF1], writes=[bF2])
            v_half(1)
            S.group("scalar", [lambda E: E.activation(F1t[:, :], F2t[:, :], AF.Exp,
                                                      scale=-S116, bias=-LN16)],
                    reads=[bF2], writes=[bF1])
            S.group("scalar", [lambda E: E.activation(eac[:, :], F2t[:, 255::256],
                                                      AF.Exp, scale=-S116)],
                    reads=[bF2], writes=[bEAC])
            S.group("scalar", [lambda E: E.activation(F2t[:, :], F2t[:, :], AF.Exp,
                                                      scale=S116)],
                    writes=[bF2])
            fns = []
            for m in range(8):
                for kc in range(8):
                    fns.append(lambda E, m=m, kc=kc, xfm=xfm: E.matmul(
                        PBIG[:, m * 256:(m + 1) * 256],
                        arena[:, A1_WQ + kc * 1024 + m * 128:A1_WQ + kc * 1024 + m * 128 + 128],
                        xfm[:, kc * 256:(kc + 1) * 256],
                        start=(kc == 0), stop=(kc == 7)))
            S.group("tensor", fns, reads=[bARL, bXF], writes=bPB)
            S.group("vector", [lambda E: E.tensor_tensor(B2t[:, :], PBIG[:, :],
                                                         F1t[:, :], ALU.mult)],
                    reads=list(bPB) + [bF1], writes=[bB2])
            bKB = [Buf(f"kb{i}_0"), Buf(f"kb{i}_1")]
            for half in range(2):
                po, pob = POt[half], bPO[half]
                fns = []
                for mm in range(4):
                    m = half * 4 + mm
                    for kc in range(8):
                        fns.append(lambda E, m=m, mm=mm, kc=kc, p=po, xfm=xfm: E.matmul(
                            p[:, mm * 256:(mm + 1) * 256],
                            arena[:, A1_WK + kc * 1024 + m * 128:A1_WK + kc * 1024 + m * 128 + 128],
                            xfm[:, kc * 256:(kc + 1) * 256],
                            start=(kc == 0), stop=(kc == 7)))
                S.group("tensor", fns, reads=[bARL, bXF], writes=[pob])
                S.group("vector", [lambda E, half=half, p=po: E.tensor_tensor(
                    B3t[:, half * 1024:(half + 1) * 1024], p[:, :],
                    F2t[:, half * 1024:(half + 1) * 1024], ALU.mult)],
                    reads=[pob, bF2], writes=[bB3])
                fns = [lambda E, kc=kc: E.tensor_scalar(
                    B4t[:, kc * 256:(kc + 1) * 256], B3t[:, kc * 256:(kc + 1) * 256],
                    eac[:, kc:kc + 1], None, ALU.mult)
                    for kc in (half * 4 + j for j in range(4))]
                S.group("vector", fns, reads=[bB3, bEAC], writes=[bKB[half]])
            for ri, r in enumerate((0, 2, 1, 3)):
                bank = ri % 2
                tb, kc0 = r // 2, (r % 2) * 4
                half = (r % 2)
                fns = []
                for jj in range(4):
                    kc = kc0 + jj
                    so = kc * 256 + tb * 128
                    fns.append(lambda E, so=so, bank=bank, o=jj * 128:
                               E.matmul(PBIG[:, bank * 512 + o:bank * 512 + o + 128],
                                        B4t[:, so:so + 128],
                                        ident[:, :], start=True, stop=True))
                S.group("tensor", fns, reads=[bCONST, bKB[half]], writes=[bPB[bank]])
                oap = B5t[:, tb * 1024 + kc0 * 128: tb * 1024 + kc0 * 128 + 512]
                S.group("scalar", [lambda E, o=oap, bank=bank: E.copy(
                    o, PBIG[:, bank * 512:bank * 512 + 512])],
                    reads=[bPB[bank]], writes=[bB5])
            ob = bO[i % 2]
            ot = Ot[i % 2]
            bSBF = sbf_bufs
            bSQs = sq_bufs
            for hh in range(2):
                S.group("vector", [lambda E, h=hh: E.tensor_copy(
                    B6t[:, (h % 2) * 1024:(h % 2) * 1024 + 1024],
                    Ss[:, 2 * h * 512:(2 * h + 2) * 512])],
                    reads=[bS], writes=[bSBF[hh % 2]])

            def scan_tail(h):
                fns = [lambda E, m4=m4, sl=(h % 2) * 1024: E.matmul(
                    PBIG[0:1, 1792:2048], onesb[:, 0:1],
                    SQ1[:, sl + m4 * 256:sl + (m4 + 1) * 256],
                    start=(m4 == 0), stop=(m4 == 3)) for m4 in range(4)]
                S.group("tensor", fns, reads=[bSQs[h % 2], bCONST], writes=[bPB[3]])
                S.group("vector", [lambda E: E.tensor_copy(
                    rows[0:1, 0:256], PBIG[0:1, 1792:2048])],
                    reads=[bPB[3]], writes=[bROWS])
                fns = [lambda E, tc=tc, h=h: E.matmul(
                    PBIG[:, 1536 + h * 2 + tc:1536 + h * 2 + tc + 1],
                    rows[0:1, tc * 128:(tc + 1) * 128], onesr[0:1, 0:1],
                    start=True, stop=True) for tc in range(2)]
                S.group("tensor", fns, reads=[bROWS, bCONST], writes=[bPB[3]])

            for h in range(4):
                po, pob = POt[h % 2], bPO[h % 2]
                fns = []
                for bi, (sb, tb) in enumerate([(0, 0), (0, 1), (1, 1)]):
                    for kc2 in range(2):
                        c = 2 * h + kc2
                        fns.append(lambda E, c=c, sb=sb, tb=tb, bi=bi, kc2=kc2:
                                   E.matmul(PBIG[:, bi * 128:(bi + 1) * 128],
                                            B3t[:, c * 256 + sb * 128:c * 256 + sb * 128 + 128],
                                            B2t[:, c * 256 + tb * 128:c * 256 + tb * 128 + 128],
                                            start=(kc2 == 0), stop=(kc2 == 1)))
                S.group("tensor", fns, reads=[bB2, bB3], writes=[bPB[0]])
                fns = [
                    lambda E: E.tensor_tensor(PMt[:, 0:128], PBIG[:, 0:128],
                                              maskut[:, :], ALU.mult),
                    lambda E: E.tensor_copy(PMt[:, 128:256], PBIG[:, 128:256]),
                    lambda E: E.tensor_tensor(PMt[:, 256:384], PBIG[:, 256:384],
                                              maskut[:, :], ALU.mult),
                ]
                S.group("vector", fns, reads=[bPB[0], bCONST], writes=[bPMm])
                for kc2 in range(2):
                    bank = 1 + kc2
                    fns = [lambda E, sb=sb, h=h, kc2=kc2, bank=bank: E.matmul(
                        PBIG[:, bank * 512:(bank + 1) * 512],
                        B5t[:, sb * 1024 + (2 * h + kc2) * 128:sb * 1024 + (2 * h + kc2) * 128 + 128],
                        V1t[:, sb * 2048 + h * 512:sb * 2048 + h * 512 + 512],
                        start=(sb == 0), stop=(sb == 1)) for sb in range(2)]
                    S.group("tensor", fns, reads=[bB5] + bV1q, writes=[bPB[bank]])
                    S.group("vector", [lambda E, h=h, kc2=kc2, bank=bank:
                                       E.scalar_tensor_tensor(
                        Ss[:, (2 * h + kc2) * 512:(2 * h + kc2 + 1) * 512],
                        Ss[:, (2 * h + kc2) * 512:(2 * h + kc2 + 1) * 512],
                        eac[:, 2 * h + kc2:2 * h + kc2 + 1],
                        PBIG[:, bank * 512:(bank + 1) * 512],
                        ALU.mult, ALU.add)],
                        reads=[bPB[bank], bEAC], writes=[bS])
                fns = []
                slot = (h % 2) * 1024
                for m4 in range(4):
                    for kc2 in range(2):
                        fns.append(lambda E, h=h, m4=m4, kc2=kc2, p=po, sl=slot: E.matmul(
                            p[:, m4 * 256:(m4 + 1) * 256],
                            B6t[:, sl + kc2 * 512 + m4 * 128:sl + kc2 * 512 + m4 * 128 + 128],
                            B2t[:, (2 * h + kc2) * 256:(2 * h + kc2 + 1) * 256],
                            start=(kc2 == 0), stop=False))
                    for bi, (tb, sb) in enumerate([(0, 0), (1, 0), (1, 1)]):
                        fns.append(lambda E, h=h, m4=m4, tb=tb, sb=sb, bi=bi, p=po:
                                   E.matmul(
                            p[:, m4 * 256 + tb * 128:m4 * 256 + tb * 128 + 128],
                            V1t[:, sb * 2048 + h * 512 + m4 * 128:sb * 2048 + h * 512 + m4 * 128 + 128],
                            PMt[:, bi * 128:(bi + 1) * 128],
                            start=False, stop=(bi == 2)))
                S.group("tensor", fns, reads=[bSBF[h % 2], bB2] + bV1q + [bPMm], writes=[pob])
                if h + 2 < 4:
                    S.group("vector", [lambda E, h2=h + 2: E.tensor_copy(
                        B6t[:, (h2 % 2) * 1024:(h2 % 2) * 1024 + 1024],
                        Ss[:, 2 * h2 * 512:(2 * h2 + 2) * 512])],
                        reads=[bS], writes=[bSBF[h % 2]])
                S.group("scalar", [lambda E, p=po, sl=slot: E.activation(
                    SQ1[:, sl:sl + 1024], p[:, :], AF.Square)],
                    reads=[pob], writes=[bSQs[h % 2]])
                S.group("scalar", [lambda E, h=h, p=po: E.copy(
                    ot[:, h * 1024:(h + 1) * 1024], p[:, :])],
                    reads=[pob], writes=[ob])
                if h > 0:
                    scan_tail(h - 1)
            scan_tail(3)
            S.group("vector", [lambda E, i=i: E.tensor_copy(
                rowsA[:, i * 8:(i + 1) * 8], PBIG[:, 1536:1544])],
                reads=[bPB[3]], writes=[bRA])
            S.group("sync", [lambda E, t=t0, o=ot: E.dma_start(
                out=orms_d[:, :, t:t + T],
                in_=o[:, :].rearrange("p (ch t) -> p ch t", ch=16))],
                reads=[ob], dma=True)

        a2_obh = [[], []]

        def arena_load_r(off, srcb, kc, ncols_per_kc, rbuf, wbuf):
            dst = arena[:, off:off + kc * ncols_per_kc].rearrange(
                "p (kc f) -> p kc f", kc=kc)
            src_ap = srcb[:, :].rearrange("(kc p) f -> p kc f", p=128)
            S.group("sync", [lambda E, d=dst, s_=src_ap: E.dma_start(out=d, in_=s_)],
                    reads=[rbuf], writes=[wbuf], dma=True)

        def load_phase_a2_weights():
            arena_load_r(A2_WG, wgbf_d, 8, 2048, bWBF["wg"], bARL)
            arena_load_r(A2_WO, wobf_d, 16, 1024, bWBF["wo"], bARH)

        def load_a2(i):
            t0 = i * T
            S.group("sync", [lambda E, d=Xt[i % 2], t=t0: E.dma_start(
                out=d[:, :].rearrange("p (tc d) -> p tc d", tc=2),
                in_=x_d[t:t + T, :].rearrange("(tc p) d -> p tc d", p=128))],
                writes=[bX[i % 2]], dma=True)
            S.group("gpsimd", [lambda E, d=[B2t, B1t][i % 2], t=t0: E.dma_start(
                out=d[:, :].rearrange("p (kc t) -> p kc t", kc=8),
                in_=xt_d[:, t:t + T].rearrange("(kc p) t -> p kc t", p=128))],
                writes=[[bB2, bB1][i % 2]], dma=True)
            S.group("sync", [lambda E, d=Ot[i % 2], t=t0: E.dma_start(
                out=d[:, :].rearrange("p (ch t) -> p ch t", ch=16),
                in_=orms_d[:, :, t:t + T])],
                writes=[bO[i % 2]] + a2_obh[i % 2], dma=True)

        def emit_rstdA():
            xcol, fbc, ya_, tt_, yb_ = (F1t[:, 0:128], F1t[:, 128:256],
                                        F1t[:, 256:384], F1t[:, 384:512],
                                        F1t[:, 512:640])
            for fn in [
                lambda E: E.tensor_scalar(xcol, rowsA[:, :], float(1.0 / 512.0),
                                          float(EPS), ALU.mult, ALU.add),
                lambda E: E.tensor_copy(fbc, xcol.bitcast(I32)),
                lambda E: E.tensor_scalar(tt_, fbc, -0.5, MAGICF, ALU.mult, ALU.add),
                lambda E: E.tensor_copy(ya_.bitcast(I32), tt_),
            ]:
                S.group("vector", [fn], reads=[bRA], writes=[bF1])
            cur, nxt = ya_, yb_
            for it in range(3):
                for fn in [
                    lambda E, c=cur: E.tensor_tensor(tt_, c, c, ALU.mult),
                    lambda E: E.tensor_tensor(tt_, tt_, xcol, ALU.mult),
                    lambda E: E.tensor_scalar(tt_, tt_, -0.5, 1.5, ALU.mult, ALU.add),
                    lambda E, c=cur, n=nxt: E.tensor_tensor(n, c, tt_, ALU.mult),
                ]:
                    S.group("vector", [fn], writes=[bF1])
                cur, nxt = nxt, cur
            S.group("vector", [lambda E, c=cur: E.tensor_copy(rstdA[:, :], c)],
                    reads=[bF1], writes=[bRS])

        def emit_a2_tile(i):
            t0 = i * T
            xb, xt = bX[i % 2], Xt[i % 2]
            ob, ot = bO[i % 2], Ot[i % 2]
            if i + 1 < NT:
                load_a2(i + 1)
            xfm = [B2t, B1t][i % 2]
            bXF = [bB2, bB1][i % 2]
            obh = [Buf(f"o2h{i}_0"), Buf(f"o2h{i}_1")]
            for qtr in range(4):
                b0 = (qtr % 2) * 2
                fns = []
                for mm in range(4):
                    m = qtr * 4 + mm
                    for kc in range(8):
                        fns.append(lambda E, m=m, mm=mm, kc=kc, b0=b0, xfm=xfm: E.matmul(
                            PBIG[:, b0 * 512 + mm * 256:b0 * 512 + (mm + 1) * 256],
                            arena[:, A2_WG + kc * 2048 + m * 128:A2_WG + kc * 2048 + m * 128 + 128],
                            xfm[:, kc * 256:(kc + 1) * 256],
                            start=(kc == 0), stop=(kc == 7)))
                S.group("tensor", fns, reads=[bARL, bXF],
                        writes=[bPB[b0], bPB[b0 + 1]])
                S.group("scalar", [lambda E, qtr=qtr, b0=b0: E.activation(
                    F1t[:, (qtr % 2) * 1024:(qtr % 2) * 1024 + 1024],
                    PBIG[:, b0 * 512:(b0 + 2) * 512], AF.Silu)],
                    reads=[bPB[b0], bPB[b0 + 1]], writes=[bF1])
                fns = [lambda E, mm=mm, qtr=qtr: E.tensor_scalar(
                    V1t[:, qtr * 1024 + mm * 256:qtr * 1024 + (mm + 1) * 256],
                    F1t[:, (qtr % 2) * 1024 + mm * 256:(qtr % 2) * 1024 + (mm + 1) * 256],
                    gnw4[:, (qtr * 4 + mm) % 4:(qtr * 4 + mm) % 4 + 1], None,
                    ALU.mult) for mm in range(4)]
                S.group("vector", fns, reads=[bF1, bCONST], writes=[bV1q[qtr]])
                if qtr % 2 == 1:
                    kh = qtr // 2
                    S.group("vector", [lambda E, o=ot, kh=kh: E.tensor_tensor(
                        o[:, kh * 2048:(kh + 1) * 2048],
                        o[:, kh * 2048:(kh + 1) * 2048],
                        V1t[:, kh * 2048:(kh + 1) * 2048], ALU.mult)],
                        reads=[bV1q[2 * kh], bV1q[2 * kh + 1], ob], writes=[obh[kh]])
            for tb in range(2):
                for h in range(4):
                    j = tb * 4 + h
                    po, pob = POt[j % 2], bPO[j % 2]
                    fns = []
                    for ng in range(2):
                        for kk in range(4):
                            kc = 4 * h + kk
                            fns.append(lambda E, tb=tb, ng=ng, kc=kc, kk=kk, p=po: E.matmul(
                                p[:, ng * 512:(ng + 1) * 512],
                                ot[:, kc * 256 + tb * 128:kc * 256 + tb * 128 + 128],
                                arena[:, A2_WO + kc * 1024 + ng * 512:A2_WO + kc * 1024 + ng * 512 + 512],
                                start=(kk == 0), stop=(kk == 3)))
                    S.group("tensor", fns, reads=[bARH, obh[h // 2]], writes=[pob])
                    rc = rstdA[:, i * 8 + h * 2 + tb:i * 8 + h * 2 + tb + 1]
                    if h == 0:
                        S.group("vector", [lambda E, tb=tb, p=po, rc=rc: E.tensor_scalar(
                            F1t[:, tb * 1024:(tb + 1) * 1024], p[:, :], rc, None,
                            ALU.mult)], reads=[pob, bRS], writes=[bF1])
                    else:
                        S.group("vector", [lambda E, tb=tb, p=po, rc=rc:
                                           E.scalar_tensor_tensor(
                            F1t[:, tb * 1024:(tb + 1) * 1024], p[:, :], rc,
                            F1t[:, tb * 1024:(tb + 1) * 1024], ALU.mult, ALU.add)],
                            reads=[pob, bRS], writes=[bF1])
            a2_obh[i % 2] = obh
            for tb in range(2):
                S.group("vector", [lambda E, tb=tb, x=xt: E.scalar_tensor_tensor(
                    F1t[:, tb * 1024:(tb + 1) * 1024],
                    F1t[:, tb * 1024:(tb + 1) * 1024], 1.0,
                    x[:, tb * 1024:(tb + 1) * 1024], ALU.mult, ALU.add,
                    accum_out=statr[:, tb:tb + 1])],
                    reads=[xb], writes=[bF1, bSTAT])
                S.group("vector", [lambda E, tb=tb: E.scalar_tensor_tensor(
                    F2t[:, tb * 1024:(tb + 1) * 1024],
                    F1t[:, tb * 1024:(tb + 1) * 1024], 1.0,
                    F1t[:, tb * 1024:(tb + 1) * 1024], ALU.mult, ALU.mult,
                    accum_out=statr[:, 2 + tb:3 + tb])],
                    reads=[bF1], writes=[bF2, bSTAT])
            emit_ln_tail(i, bF1, F1t, bY[i % 2], Yt[i % 2])
            S.group("sync", [lambda E, t=t0, y=Yt[i % 2]: E.dma_start(
                out=ybuf_d[t:t + T, :].rearrange("(tc p) d -> p tc d", p=128),
                in_=y[:, :].rearrange("p (tc d) -> p tc d", tc=2))],
                reads=[bY[i % 2]], dma=True)

        def emit_ln_tail(i, srcb, srct, dstb, dstt):
            inv = float(1.0 / D)
            S.group("vector", [lambda E: E.tensor_scalar(
                statr[:, 4:6], statr[:, 0:2], inv, None, ALU.mult)], writes=[bSTAT])
            S.group("vector", [lambda E: E.tensor_scalar(
                statr[:, 6:8], statr[:, 2:4], inv, None, ALU.mult)], writes=[bSTAT])
            S.group("vector", [lambda E: E.tensor_tensor(
                statr[:, 8:10], statr[:, 4:6], statr[:, 4:6], ALU.mult)], writes=[bSTAT])
            S.group("vector", [lambda E: E.tensor_tensor(
                statr[:, 10:12], statr[:, 6:8], statr[:, 8:10], ALU.subtract)], writes=[bSTAT])
            S.group("vector", [lambda E: E.tensor_scalar(
                statr[:, 24:26], statr[:, 4:6], -1.0, None, ALU.mult)], writes=[bSTAT])
            x, fb, y0, ya, tt, yb = (statr[:, 12:14], statr[:, 14:16],
                                     statr[:, 16:18], statr[:, 18:20],
                                     statr[:, 20:22], statr[:, 22:24])
            for fn in [
                lambda E: E.tensor_scalar(x, statr[:, 10:12], float(EPS), None, ALU.add),
                lambda E: E.tensor_copy(fb, x.bitcast(I32)),
                lambda E: E.tensor_scalar(y0, fb, -0.5, MAGICF, ALU.mult, ALU.add),
                lambda E: E.tensor_copy(ya.bitcast(I32), y0),
            ]:
                S.group("vector", [fn], writes=[bSTAT])
            cur = ya
            for it in range(3):
                nxt = yb if cur is ya else ya
                for fn in [
                    lambda E, c=cur: E.tensor_tensor(tt, c, c, ALU.mult),
                    lambda E: E.tensor_tensor(tt, tt, x, ALU.mult),
                    lambda E: E.tensor_scalar(tt, tt, -0.5, 1.5, ALU.mult, ALU.add),
                    lambda E, c=cur, n=nxt: E.tensor_tensor(n, c, tt, ALU.mult),
                ]:
                    S.group("vector", [fn], writes=[bSTAT])
                cur = nxt
            rcol = statr[:, 18:20] if cur is ya else statr[:, 22:24]
            fns = [lambda E, tb=tb, rc=rcol: E.tensor_scalar(
                dstt[:, tb * 1024:(tb + 1) * 1024], srct[:, tb * 1024:(tb + 1) * 1024],
                statr[:, 24 + tb:25 + tb], rc[:, tb:tb + 1],
                ALU.add, ALU.mult) for tb in range(2)]
            S.group("vector", fns, reads=[srcb, bSTAT], writes=[dstb])

        def load_phase_b_weights():
            arena_load_r(B_WR, wrtbf_d, 8, 1024, bWBF["wrt"], bARL)
            arena_load_r(B_WK, wktbf_d, 8, 1024, bWBF["wkt"], bARL)
            arena_load_r(B_WV, wvtbf_d, 8, 1024, bWBF["wvt"], bARH)

        def load_b(i):
            t0 = i * T
            S.group("gpsimd", [lambda E, d=Yt[i % 2], t=t0: E.dma_start(
                out=d[:, :].rearrange("p (tc d) -> p tc d", tc=2),
                in_=ybuf_d[t:t + T, :].rearrange("(tc p) d -> p tc d", p=128))],
                writes=[bY[i % 2]], dma=True)

        def b_front(i):
            yb, yt = bY[i % 2], Yt[i % 2]
            xrb, xkb = [bB4, bB4u][i % 2], [bB5, bB5u][i % 2]
            xrt, xkt = [B4t, B4u][i % 2], [B5t, B5u][i % 2]
            S.group("scalar", [lambda E, s=yt: E.copy(B1t[:, :], s[:, :])],
                    reads=[yb], writes=[bB1])
            for r in range(4):
                bank = r % 2
                fns = []
                for jj in range(4):
                    j = r * 4 + jj
                    tc, kc = j // 8, j % 8
                    so = tc * 1024 + kc * 128
                    fns.append(lambda E, so=so, bank=bank, o=jj * 128:
                               E.matmul(PBIG[:, bank * 512 + o:bank * 512 + o + 128],
                                        B1t[:, so:so + 128],
                                        ident[:, :], start=True, stop=True))
                S.group("tensor", fns, reads=[bCONST, bB1], writes=[bPB[bank]])
                tcr, kc0 = (r * 4) // 8, (r * 4) % 8
                out_ap = B2t[:, :].rearrange("p (kc t) -> p kc t", kc=8)[
                    :, kc0:kc0 + 4, tcr * 128:(tcr + 1) * 128]
                S.group("scalar", [lambda E, o=out_ap, bank=bank: E.copy(
                    o, PBIG[:, bank * 512:bank * 512 + 512].rearrange(
                        "p (b t) -> p b t", b=4))],
                    reads=[bPB[bank]], writes=[bB2])
            pc_in, pc_out = bPCOL[i % 2], bPCOL[(i + 1) % 2]
            fns = [lambda E, kc=kc: E.copy(
                B3t[:, kc * 256 + 1:(kc + 1) * 256],
                B2t[:, kc * 256:(kc + 1) * 256 - 1]) for kc in range(8)]
            fns.append(lambda E, p=PCt[i % 2]: E.copy(B3t[:, 0::256], p[:, :]))
            S.group("scalar", fns, reads=[bB2, pc_in], writes=[bB3])
            S.group("vector", [lambda E, p=PCt[(i + 1) % 2]: E.tensor_copy(
                p[:, :], B2t[:, 255::256])], reads=[bB2], writes=[pc_out])
            S.group("vector", [lambda E: E.tensor_tensor(B6t[:, :], B2t[:, :],
                                                         B3t[:, :], ALU.subtract)],
                    reads=[bB2, bB3], writes=[bB6])
            fns = [lambda E, kc=kc, o=xrt: E.scalar_tensor_tensor(
                o[:, kc * 256:(kc + 1) * 256], B6t[:, kc * 256:(kc + 1) * 256],
                murc[:, kc:kc + 1], B3t[:, kc * 256:(kc + 1) * 256],
                ALU.mult, ALU.add) for kc in range(8)]
            S.group("vector", fns, reads=[bB6, bB3, bCONST], writes=[xrb])
            fns = [lambda E, kc=kc, o=xkt: E.scalar_tensor_tensor(
                o[:, kc * 256:(kc + 1) * 256], B6t[:, kc * 256:(kc + 1) * 256],
                mukc[:, kc:kc + 1], B3t[:, kc * 256:(kc + 1) * 256],
                ALU.mult, ALU.add) for kc in range(8)]
            S.group("vector", fns, reads=[bB6, bB3, bCONST], writes=[xkb])

        def emit_b_tile(i):
            t0 = i * T
            yb, yt = bY[i % 2], Yt[i % 2]
            xrb, xkb = [bB4, bB4u][i % 2], [bB5, bB5u][i % 2]
            xrt, xkt = [B4t, B4u][i % 2], [B5t, B5u][i % 2]
            kab = [bB4, bB4u][i % 2]
            kat = [B4t, B4u][i % 2]
            if i + 1 < NT:
                load_b(i + 1)
            for tb in range(2):
                po, pob = POt[tb], bPO[tb]
                fns = []
                for ng in range(2):
                    for kc in range(8):
                        fns.append(lambda E, tb=tb, ng=ng, kc=kc, p=po, x=xrt: E.matmul(
                            p[:, ng * 512:(ng + 1) * 512],
                            x[:, kc * 256 + tb * 128:kc * 256 + tb * 128 + 128],
                            arena[:, B_WR + kc * 1024 + ng * 512:B_WR + kc * 1024 + ng * 512 + 512],
                            start=(kc == 0), stop=(kc == 7)))
                S.group("tensor", fns, reads=[bARL, xrb], writes=[pob])
                S.group("scalar", [lambda E, tb=tb, p=po: E.activation(
                    F2t[:, tb * 1024:(tb + 1) * 1024], p[:, :], AF.Sigmoid)],
                    reads=[pob], writes=[bF2])
            fns = []
            for m in range(8):
                for kc in range(8):
                    fns.append(lambda E, m=m, kc=kc, x=xkt: E.matmul(
                        PBIG[:, m * 256:(m + 1) * 256],
                        arena[:, B_WK + kc * 1024 + m * 128:B_WK + kc * 1024 + m * 128 + 128],
                        x[:, kc * 256:(kc + 1) * 256],
                        start=(kc == 0), stop=(kc == 7)))
            S.group("tensor", fns, reads=[bARL, xkb], writes=bPB)
            S.group("scalar", [lambda E: E.activation(B6t[:, :], PBIG[:, :],
                                                      AF.Relu)],
                    reads=list(bPB), writes=[bB6])
            S.group("vector", [lambda E, o=kat: E.tensor_tensor(o[:, :], B6t[:, :],
                                                                B6t[:, :], ALU.mult)],
                    reads=[bB6], writes=[kab])
            if i + 1 < NT:
                b_front(i + 1)
            for tb in range(2):
                po, pob = POt[tb], bPO[tb]
                fns = []
                for ng in range(2):
                    for kc in range(8):
                        fns.append(lambda E, tb=tb, ng=ng, kc=kc, p=po, x=kat: E.matmul(
                            p[:, ng * 512:(ng + 1) * 512],
                            x[:, kc * 256 + tb * 128:kc * 256 + tb * 128 + 128],
                            arena[:, B_WV + kc * 1024 + ng * 512:B_WV + kc * 1024 + ng * 512 + 512],
                            start=(kc == 0), stop=(kc == 7)))
                S.group("tensor", fns, reads=[bARH, kab], writes=[pob])
                S.group("vector", [lambda E, tb=tb, p=po: E.tensor_tensor(
                    F1t[:, tb * 1024:(tb + 1) * 1024], p[:, :],
                    F2t[:, tb * 1024:(tb + 1) * 1024], ALU.mult)],
                    reads=[pob, bF2], writes=[bF1])
                S.group("vector", [lambda E, tb=tb, y=yt: E.scalar_tensor_tensor(
                    F1t[:, tb * 1024:(tb + 1) * 1024],
                    F1t[:, tb * 1024:(tb + 1) * 1024], 1.0,
                    y[:, tb * 1024:(tb + 1) * 1024], ALU.mult, ALU.add,
                    accum_out=statr[:, tb:tb + 1])],
                    reads=[yb], writes=[bF1, bSTAT])
                S.group("vector", [lambda E, tb=tb: E.scalar_tensor_tensor(
                    F2t[:, tb * 1024:(tb + 1) * 1024],
                    F1t[:, tb * 1024:(tb + 1) * 1024], 1.0,
                    F1t[:, tb * 1024:(tb + 1) * 1024], ALU.mult, ALU.mult,
                    accum_out=statr[:, 2 + tb:3 + tb])],
                    reads=[bF1], writes=[bF2, bSTAT])
            emit_ln_tail(i, bF1, F1t, bX[i % 2], Xt[i % 2])
            S.group("sync", [lambda E, t=t0, o=Xt[i % 2]: E.dma_start(
                out=out_d[t:t + T, :].rearrange("(tc p) d -> p tc d", p=128),
                in_=o[:, :].rearrange("p (tc d) -> p tc d", tc=2))],
                reads=[bX[i % 2]], dma=True)

        if "A1" in phases:
            load_phase_a1_weights()
            load_a1_x(0)
            for i in range(NT):
                emit_a1_tile(i)
        if "A2" in phases:
            if "A1" not in phases:
                for j in range(len(wconv_list)):
                    emit_wconv(j)
            load_a2(0)
            load_phase_a2_weights()
            emit_rstdA()
            for i in range(NT):
                emit_a2_tile(i)
        if "B" in phases:
            for j in range(len(wconv_list)):
                if wconv_list[j][0] not in bWBF:
                    emit_wconv(j)
            load_b(0)
            load_phase_b_weights()
            b_front(0)
            for i in range(NT):
                emit_b_tile(i)

        S.raw_wait("gpsimd", "sync", S.cnt["sync"])
        if S.cnt["gpsimd"]:
            S.q["gpsimd"].append(
                lambda E, v=S.cnt["gpsimd"]: E.wait_ge(S.sem["gpsimd"], v))

        def runner(lst):
            def f(E):
                for fn in lst:
                    fn(E)
            return f

        block.tensor(runner(S.q["tensor"]))
        block.vector(runner(S.q["vector"]))
        block.scalar(runner(S.q["scalar"]))
        block.gpsimd(runner(S.q["gpsimd"]))
        block.sync(runner(S.q["sync"]))

    return nc


def host_inputs(inputs, core):
    f32 = np.float32
    x = np.ascontiguousarray(inputs["L0_attn_x"][core])
    d = {
        "x": x,
        "xt": np.ascontiguousarray(x.T),
        "wq": inputs["L0_attn_Wq"],
        "wk": inputs["L0_attn_Wk"],
        "wv": inputs["L0_attn_Wv"],
        "wg": inputs["L0_attn_Wg"],
        "gw1": inputs["L0_attn_gk_w1"],
        "gw2": inputs["L0_attn_gk_w2"],
        "wo": inputs["L0_attn_Wo"],
        "wrt": np.ascontiguousarray(inputs["L0_ffn_W_r"].T),
        "wkt": np.ascontiguousarray(inputs["L0_ffn_W_k"].T),
        "wvt": np.ascontiguousarray(inputs["L0_ffn_W_v"].T),
        "ident": np.eye(128, dtype=f32),
        "maskut": np.triu(np.ones((128, 128), dtype=f32)),
        "onesb": np.ones((128, 8), dtype=f32),
        "onesr": np.ones((1, 128), dtype=f32),
        "b2c": np.ascontiguousarray(inputs["L0_attn_gk_b2"].reshape(8, 128).T),
        "gnw4": np.ascontiguousarray(inputs["L0_attn_g_norm_w"].reshape(4, 128).T),
        "murc": np.ascontiguousarray(inputs["L0_ffn_mu_r"].reshape(8, 128).T),
        "mukc": np.ascontiguousarray(inputs["L0_ffn_mu_k"].reshape(8, 128).T),
    }
    return d


def kernel(**inputs):
    nc = build_program(paranoid=True)
    in_maps = [host_inputs(inputs, c) for c in range(8)]
    res = run_bass_kernel_spmd(nc, in_maps, core_ids=list(range(8)))
    out = np.stack([res.results[c]["out"] for c in range(8)], axis=0)
    return out.astype(np.float32)


# version 20
# speedup vs baseline: 1.0504x; 1.0504x over previous
import numpy as np
import ml_dtypes
import concourse.bass as bass
import concourse.mybir as mybir
from concourse.bass_utils import run_bass_kernel_spmd

F32 = mybir.dt.float32
BF16 = mybir.dt.bfloat16
AF = mybir.ActivationFunctionType
ALU = mybir.AluOpType
I32 = mybir.dt.int32
MAGICF = float(0x5f3759df)

L = 4096
D = 1024
T = 256
NT = L // T
H = 4
DK = 256
DV = 512
EPS = np.float32(1e-5)
LN16 = np.float32(np.log(16.0))
S116 = np.float32(1.0 / 16.0)

ENGS = ("tensor", "vector", "scalar", "gpsimd", "sync", "pool")


class Buf:
    __slots__ = ("name", "ready", "users")

    def __init__(self, name):
        self.name = name
        self.ready = None
        self.users = []


class Sch:

    def __init__(self, nc, paranoid=False):
        self.nc = nc
        self.paranoid = paranoid
        self.q = {e: [] for e in ENGS}
        self.cnt = {e: 0 for e in ENGS}
        self.sem = {}
        self.lastwait = {e: {} for e in ENGS}

    def need(self, eng, tok):
        if tok is None:
            return
        peng, val = tok
        if peng == eng:
            if not (self.paranoid and eng in ("vector", "scalar", "pool")):
                return
        if peng == "pool" and eng == "gpsimd":
            return
        if self.lastwait[eng].get(peng, 0) >= val:
            return
        self.lastwait[eng][peng] = val
        self.q["gpsimd" if eng == "pool" else eng].append(
            lambda E, p=peng, v=val: E.wait_ge(self.sem[p], v))

    def group(self, eng, fns, reads=(), writes=(), dma=False):
        for b in reads:
            self.need(eng, b.ready)
        for b in writes:
            self.need(eng, b.ready)
            for u in b.users:
                self.need(eng, u)
        phys = "gpsimd" if eng == "pool" else eng
        n = 16 if dma else 1
        if dma and self.cnt[eng]:
            self.q[phys].append(
                lambda E, e=eng, v=self.cnt[eng]: E.wait_ge(self.sem[e], v))
        for fn in fns[:-1]:
            self.q[phys].append(fn)
        self.cnt[eng] += n
        val = self.cnt[eng]

        def last(E, fn=fns[-1], e=eng, k=n):
            fn(E).then_inc(self.sem[e], k)

        self.q[phys].append(last)
        tok = (eng, val)
        for b in reads:
            b.users.append(tok)
        for b in writes:
            b.ready = tok
            b.users = []
        return tok

    def raw_wait(self, eng, peng, val):
        self.q[eng].append(lambda E, p=peng, v=val: E.wait_ge(self.sem[p], v))


def build_program(phases=("A1", "A2", "B"), paranoid=False):
    nc = bass.Bass("TRN2", target_bir_lowering=False)
    dt = nc.dram_tensor
    x_d = dt("x", [L, D], F32, kind="ExternalInput")
    xt_d = dt("xt", [D, L], F32, kind="ExternalInput")
    wq_d = dt("wq", [D, 1024], F32, kind="ExternalInput")
    wk_d = dt("wk", [D, 1024], F32, kind="ExternalInput")
    wv_d = dt("wv", [D, 2048], F32, kind="ExternalInput")
    wg_d = dt("wg", [D, 2048], F32, kind="ExternalInput")
    gw1_d = dt("gw1", [D, 16], F32, kind="ExternalInput")
    gw2_d = dt("gw2", [16, D], F32, kind="ExternalInput")
    wo_d = dt("wo", [2048, D], F32, kind="ExternalInput")
    wrt_d = dt("wrt", [D, D], F32, kind="ExternalInput")
    wkt_d = dt("wkt", [D, D], F32, kind="ExternalInput")
    wvt_d = dt("wvt", [D, D], F32, kind="ExternalInput")
    ident_d = dt("ident", [128, 128], F32, kind="ExternalInput")
    mask_d = dt("maskut", [128, 128], F32, kind="ExternalInput")
    onesb_d = dt("onesb", [128, 8], F32, kind="ExternalInput")
    onesr_d = dt("onesr", [1, 128], F32, kind="ExternalInput")
    b2c_d = dt("b2c", [128, 8], F32, kind="ExternalInput")
    gnw4_d = dt("gnw4", [128, 4], F32, kind="ExternalInput")
    murc_d = dt("murc", [128, 8], F32, kind="ExternalInput")
    mukc_d = dt("mukc", [128, 8], F32, kind="ExternalInput")

    wgbf_d = dt("wgbf", [D, 2048], BF16)
    wobf_d = dt("wobf", [2048, D], BF16)
    wrtbf_d = dt("wrtbf", [D, D], BF16)
    wktbf_d = dt("wktbf", [D, D], BF16)
    wvtbf_d = dt("wvtbf", [D, D], BF16)
    if "A1" in phases:
        orms_d = dt("orms", [128, 16, L], BF16, **({} if "A2" in phases else {"kind": "ExternalOutput"}))
    else:
        orms_d = dt("orms", [128, 16, L], BF16, kind="ExternalInput")
    if "A2" in phases:
        ybuf_d = dt("ybuf", [L, D], F32, **({} if "B" in phases else {"kind": "ExternalOutput"}))
    elif "B" in phases:
        ybuf_d = dt("ybuf", [L, D], F32, kind="ExternalInput")
    if "B" in phases:
        out_d = dt("out", [L, D], F32, kind="ExternalOutput")

    AR_COLS = 32896

    from contextlib import ExitStack
    with ExitStack() as ES:
        def sb(name, shape, dt_):
            return ES.enter_context(nc.sbuf_tensor(name, shape, dt_))

        def ps(name, shape):
            return ES.enter_context(nc.psum_tensor(name, shape, F32))

        arena = sb("arena", [128, AR_COLS], BF16)
        gw2s = sb("gw2s", [16, 1024], BF16)
        z1b = sb("z1b", [16, 256], BF16)
        ident = sb("ident_s", [128, 128], BF16)
        maskut = sb("maskut_s", [128, 128], BF16)
        onesb = sb("onesb_s", [128, 8], BF16)
        onesr = sb("onesr_s", [1, 128], F32)
        b2c = sb("b2c_s", [128, 8], F32)
        gnw4 = sb("gnw4_s", [128, 4], F32)
        murc = sb("murc_s", [128, 8], F32)
        mukc = sb("mukc_s", [128, 8], F32)
        b2n = sb("b2n_s", [128, 8], F32)
        eac = sb("eac", [128, 8], F32)
        rows = sb("rows", [1, 1024], F32)
        bcs = sb("bcs", [128, 256], F32)
        statr = sb("statr", [128, 32], F32)
        rowsA = sb("rowsA", [128, 128], F32)
        rstdA = sb("rstdA", [128, 128], F32)
        pcol0 = sb("pcol0", [128, 8], BF16)
        pcol1 = sb("pcol1", [128, 8], BF16)
        F1t = sb("F1", [128, 2048], F32)
        F2t = sb("F2", [128, 2048], F32)
        B1t = sb("B1", [128, 2048], BF16)
        B1u = sb("B1u", [128, 2048], BF16)
        B4u = sb("B4u", [128, 2048], BF16)
        B5u = sb("B5u", [128, 2048], BF16)
        SQ1 = sb("SQ1", [128, 2048], BF16)
        B2t = sb("B2", [128, 2048], BF16)
        B3t = sb("B3", [128, 2048], BF16)
        B4t = sb("B4", [128, 2048], BF16)
        B5t = sb("B5", [128, 2048], BF16)
        B6t = sb("B6", [128, 2048], BF16)
        V1t = sb("V1", [128, 4096], BF16)
        PMt = sb("PM", [128, 384], BF16)
        O0 = sb("O0", [128, 4096], BF16)
        O1 = sb("O1", [128, 4096], BF16)
        X0 = sb("X0", [128, 2048], F32)
        X1 = sb("X1", [128, 2048], F32)
        Y0 = sb("Y0", [128, 2048], F32)
        Y1 = sb("Y1", [128, 2048], F32)
        Ss = sb("Ss", [128, 4096], F32)
        XT0 = X0[:, 0:1024].bitcast(BF16)
        XT1 = X1[:, 0:1024].bitcast(BF16)
        PBIG = ps("PBIG", [128, 2048])
        PO1 = ps("PO1", [128, 1024])
        PO2 = ps("PO2", [128, 1024])
        s_te = ES.enter_context(nc.semaphore("s_te"))
        s_ve = ES.enter_context(nc.semaphore("s_ve"))
        s_se = ES.enter_context(nc.semaphore("s_se"))
        s_gp = ES.enter_context(nc.semaphore("s_gp"))
        s_sy = ES.enter_context(nc.semaphore("s_sy"))
        s_pl = ES.enter_context(nc.semaphore("s_pl"))
        block = ES.enter_context(nc.Block())

        S = Sch(nc, paranoid=paranoid)
        sems = {"tensor": s_te, "vector": s_ve, "scalar": s_se,
                "gpsimd": s_gp, "sync": s_sy, "pool": s_pl}
        S.sem = sems

        bWBF = {}
        bARL = Buf("arenaL")
        bARH = Buf("arenaH")
        bGW2 = Buf("gw2")
        bZ1B = Buf("z1b")
        bCONST = Buf("const")
        bEAC = Buf("eac")
        bROWS = Buf("rows")
        bBCS = Buf("bcs")
        bSTAT = Buf("statr")
        bRA = Buf("rowsA")
        bRS = Buf("rstdA")
        bPCOL = [Buf("pcol0"), Buf("pcol1")]
        bF1 = Buf("F1"); bF2 = Buf("F2")
        bB1 = Buf("B1"); bB1u = Buf("B1u"); bSQ = Buf("SQ1")
        bB4u = Buf("B4u"); bB5u = Buf("B5u")
        sbf_bufs = [Buf("sbf0"), Buf("sbf1")]
        sq_bufs = [Buf("sq0"), Buf("sq1")]
        bB2 = Buf("B2"); bB3 = Buf("B3")
        bB4 = Buf("B4"); bB5 = Buf("B5"); bB6 = Buf("B6")
        bV1q = [Buf("V1q0"), Buf("V1q1"), Buf("V1q2"), Buf("V1q3")]
        bPMm = Buf("PM")
        bO = [Buf("O0"), Buf("O1")]
        bX = [Buf("X0"), Buf("X1")]
        bXT = bX
        bY = [Buf("Y0"), Buf("Y1")]
        bS = Buf("Ss")
        bPB = [Buf("PB0"), Buf("PB1"), Buf("PB2"), Buf("PB3")]
        bPO = [Buf("PO1"), Buf("PO2")]
        POt = [PO1, PO2]
        XTt = [XT0, XT1]
        Ot = [O0, O1]
        Xt = [X0, X1]
        Yt = [Y0, Y1]
        PCt = [pcol0, pcol1]

        A1_WQ, A1_WK, A1_WV, A1_GW1 = 0, 8192, 16384, 32768
        A2_WG, A2_WO = 0, 16384
        B_WR, B_WK, B_WV = 0, 8192, 16384

        def load_cast(dst_ap, src_ap):
            return lambda E, d=dst_ap, s=src_ap: E.dma_start(out=d, in_=s)

        S.group("gpsimd", [load_cast(ident[:, :], ident_d[:, :])], writes=[bCONST], dma=True)
        S.group("gpsimd", [load_cast(maskut[:, :], mask_d[:, :])], writes=[bCONST], dma=True)
        S.group("gpsimd", [load_cast(onesb[:, :], onesb_d[:, :])], writes=[bCONST], dma=True)
        S.group("sync", [load_cast(onesr[:, :], onesr_d[:, :])], writes=[bCONST], dma=True)
        S.group("sync", [load_cast(b2c[:, :], b2c_d[:, :])], writes=[bCONST], dma=True)
        S.group("sync", [load_cast(gnw4[:, :], gnw4_d[:, :])], writes=[bCONST], dma=True)
        S.group("sync", [load_cast(murc[:, :], murc_d[:, :])], writes=[bCONST], dma=True)
        S.group("sync", [load_cast(mukc[:, :], mukc_d[:, :])], writes=[bCONST], dma=True)
        S.group("vector", [lambda E: E.tensor_scalar(b2n[:, :], b2c[:, :], -1.0,
                                                     None, ALU.mult)],
                reads=[bCONST], writes=[bCONST])
        S.group("vector", [lambda E: E.memset(Ss[:, :], 0.0)], writes=[bS])
        S.group("vector", [lambda E: E.memset(rowsA[:, :], 1.0)], writes=[bRA])
        S.group("vector", [lambda E: E.memset(pcol0[:, :], 0.0)], writes=[bPCOL[0]])

        def arena_load(off, src, kc, ncols_per_kc):
            dst = arena[:, off:off + kc * ncols_per_kc].rearrange(
                "p (kc f) -> p kc f", kc=kc)
            src_ap = src[:, :].rearrange("(kc p) f -> p kc f", p=128)
            return lambda E, d=dst, s=src_ap: E.dma_start(out=d, in_=s)

        wconv_list = [("wg", wg_d, wgbf_d), ("wo", wo_d, wobf_d),
                      ("wrt", wrt_d, wrtbf_d), ("wkt", wkt_d, wktbf_d),
                      ("wvt", wvt_d, wvtbf_d)]

        def emit_wconv(j):
            name, src, dst = wconv_list[j]
            bWBF[name] = Buf("bf_" + name)
            S.group("gpsimd", [lambda E, s_=src, d_=dst: E.dma_start(
                out=d_[:, :], in_=s_[:, :])], writes=[bWBF[name]], dma=True)

        def load_phase_a1_weights():
            S.group("gpsimd", [arena_load(A1_WQ, wq_d, 8, 1024)], writes=[bARL], dma=True)
            S.group("gpsimd", [arena_load(A1_WK, wk_d, 8, 1024)], writes=[bARL], dma=True)
            S.group("gpsimd", [arena_load(A1_WV, wv_d, 8, 2048)], writes=[bARH], dma=True)
            S.group("gpsimd", [arena_load(A1_GW1, gw1_d, 8, 16)], writes=[bARH], dma=True)
            S.group("gpsimd", [load_cast(gw2s[:, :], gw2_d[:, :])], writes=[bGW2], dma=True)

        src_bufs = {}
        dst_bufs = {}

        def load_a1_x(i):
            t0 = i * T
            S.group("gpsimd",
                    [lambda E, d=[B1t, B1u][i % 2], t=t0: E.dma_start(
                        out=d[:, :].rearrange("p (kc t) -> p kc t", kc=8),
                        in_=xt_d[:, t:t + T].rearrange("(kc p) t -> p kc t", p=128))],
                    writes=[[bB1, bB1u][i % 2]], dma=True)

        def emit_a1_tile(i):
            t0 = i * T
            if i + 1 < NT:
                load_a1_x(i + 1)
            if "A2" in phases and i < len(wconv_list):
                emit_wconv(i)
            xfm = [B1t, B1u][i % 2]
            bXF = [bB1, bB1u][i % 2]
            fns = [lambda E, kc=kc, xfm=xfm: E.matmul(
                PO1[0:16, 0:256], arena[:, A1_GW1 + kc * 16:A1_GW1 + kc * 16 + 16],
                xfm[:, kc * 256:(kc + 1) * 256], start=(kc == 0), stop=(kc == 7))
                for kc in range(8)]
            S.group("tensor", fns, reads=[bARH, bXF], writes=[bPO[0]])
            S.group("vector", [lambda E: E.tensor_copy(z1b[:, :], PO1[0:16, 0:256])],
                    reads=[bPO[0]], writes=[bZ1B])
            fns = [lambda E, m=m: E.matmul(
                PBIG[:, m * 256:(m + 1) * 256], gw2s[0:16, m * 128:(m + 1) * 128],
                z1b[0:16, :], start=True, stop=True) for m in range(8)]
            S.group("tensor", fns, reads=[bGW2, bZ1B], writes=bPB)
            for bank in range(4):
                fns = [lambda E, kc=kc: E.activation(
                    F2t[:, kc * 256:(kc + 1) * 256], PBIG[:, kc * 256:(kc + 1) * 256],
                    AF.Exp, scale=np.float32(-1.0), bias=b2n[:, kc:kc + 1])
                    for kc in (2 * bank, 2 * bank + 1)]
                S.group("scalar", fns, reads=[bPB[bank], bCONST], writes=[bF2])
            def v_half(tb):
                for grp in range(4):
                    j = tb * 4 + grp
                    bank = j % 4
                    fns = [lambda E, kc=kc, tb=tb, grp=grp, bank=bank, xfm=xfm: E.matmul(
                        PBIG[:, bank * 512:(bank + 1) * 512],
                        xfm[:, kc * 256 + tb * 128:kc * 256 + tb * 128 + 128],
                        arena[:, A1_WV + kc * 2048 + grp * 512:A1_WV + kc * 2048 + grp * 512 + 512],
                        start=(kc == 0), stop=(kc == 7)) for kc in range(8)]
                    S.group("tensor", fns, reads=[bARH, bXF], writes=[bPB[bank]])
                    oap = V1t[:, tb * 2048 + grp * 512: tb * 2048 + grp * 512 + 512]
                    S.group("vector", [lambda E, o=oap, b=bank: E.tensor_copy(
                        o, PBIG[:, b * 512:(b + 1) * 512])],
                        reads=[bPB[bank]], writes=[bV1q[tb * 2 + grp // 2]])

            v_half(0)
            S.group("scalar", [lambda E: E.activation(F1t[:, :], F2t[:, :], AF.Ln,
                                                      bias=np.float32(1.0))],
                    reads=[bF2], writes=[bF1])
            fns = [lambda E, kc=kc: E.tensor_tensor_scan(
                F2t[:, kc * 256:(kc + 1) * 256], F1t[:, kc * 256:(kc + 1) * 256],
                F1t[:, kc * 256:(kc + 1) * 256], 0.0, ALU.add, ALU.bypass)
                for kc in range(8)]
            S.group("vector", fns, reads=[bF1], writes=[bF2])
            v_half(1)
            S.group("scalar", [lambda E: E.activation(F1t[:, :], F2t[:, :], AF.Exp,
                                                      scale=-S116, bias=-LN16)],
                    reads=[bF2], writes=[bF1])
            S.group("scalar", [lambda E: E.activation(eac[:, :], F2t[:, 255::256],
                                                      AF.Exp, scale=-S116)],
                    reads=[bF2], writes=[bEAC])
            S.group("scalar", [lambda E: E.activation(F2t[:, :], F2t[:, :], AF.Exp,
                                                      scale=S116)],
                    writes=[bF2])
            fns = []
            for m in range(8):
                for kc in range(8):
                    fns.append(lambda E, m=m, kc=kc, xfm=xfm: E.matmul(
                        PBIG[:, m * 256:(m + 1) * 256],
                        arena[:, A1_WQ + kc * 1024 + m * 128:A1_WQ + kc * 1024 + m * 128 + 128],
                        xfm[:, kc * 256:(kc + 1) * 256],
                        start=(kc == 0), stop=(kc == 7)))
            S.group("tensor", fns, reads=[bARL, bXF], writes=bPB)
            S.group("vector", [lambda E: E.tensor_tensor(B2t[:, :], PBIG[:, :],
                                                         F1t[:, :], ALU.mult)],
                    reads=list(bPB) + [bF1], writes=[bB2])
            bKB = [Buf(f"kb{i}_0"), Buf(f"kb{i}_1")]
            for half in range(2):
                po, pob = POt[half], bPO[half]
                fns = []
                for mm in range(4):
                    m = half * 4 + mm
                    for kc in range(8):
                        fns.append(lambda E, m=m, mm=mm, kc=kc, p=po, xfm=xfm: E.matmul(
                            p[:, mm * 256:(mm + 1) * 256],
                            arena[:, A1_WK + kc * 1024 + m * 128:A1_WK + kc * 1024 + m * 128 + 128],
                            xfm[:, kc * 256:(kc + 1) * 256],
                            start=(kc == 0), stop=(kc == 7)))
                S.group("tensor", fns, reads=[bARL, bXF], writes=[pob])
                S.group("vector", [lambda E, half=half, p=po: E.tensor_tensor(
                    B3t[:, half * 1024:(half + 1) * 1024], p[:, :],
                    F2t[:, half * 1024:(half + 1) * 1024], ALU.mult)],
                    reads=[pob, bF2], writes=[bB3])
                fns = [lambda E, kc=kc: E.tensor_scalar(
                    B4t[:, kc * 256:(kc + 1) * 256], B3t[:, kc * 256:(kc + 1) * 256],
                    eac[:, kc:kc + 1], None, ALU.mult)
                    for kc in (half * 4 + j for j in range(4))]
                S.group("vector", fns, reads=[bB3, bEAC], writes=[bKB[half]])
            for ri, r in enumerate((0, 2, 1, 3)):
                bank = ri % 2
                tb, kc0 = r // 2, (r % 2) * 4
                half = (r % 2)
                fns = []
                for jj in range(4):
                    kc = kc0 + jj
                    so = kc * 256 + tb * 128
                    fns.append(lambda E, so=so, bank=bank, o=jj * 128:
                               E.matmul(PBIG[:, bank * 512 + o:bank * 512 + o + 128],
                                        B4t[:, so:so + 128],
                                        ident[:, :], start=True, stop=True))
                S.group("tensor", fns, reads=[bCONST, bKB[half]], writes=[bPB[bank]])
                oap = B5t[:, tb * 1024 + kc0 * 128: tb * 1024 + kc0 * 128 + 512]
                S.group("scalar", [lambda E, o=oap, bank=bank: E.copy(
                    o, PBIG[:, bank * 512:bank * 512 + 512])],
                    reads=[bPB[bank]], writes=[bB5])
            ob = bO[i % 2]
            ot = Ot[i % 2]
            bSBF = sbf_bufs
            bSQs = sq_bufs
            for hh in range(2):
                S.group("vector", [lambda E, h=hh: E.tensor_copy(
                    B6t[:, (h % 2) * 1024:(h % 2) * 1024 + 1024],
                    Ss[:, 2 * h * 512:(2 * h + 2) * 512])],
                    reads=[bS], writes=[bSBF[hh % 2]])

            def scan_tail(h):
                fns = [lambda E, m4=m4, sl=(h % 2) * 1024: E.matmul(
                    PBIG[0:1, 1792:2048], onesb[:, 0:1],
                    SQ1[:, sl + m4 * 256:sl + (m4 + 1) * 256],
                    start=(m4 == 0), stop=(m4 == 3)) for m4 in range(4)]
                S.group("tensor", fns, reads=[bSQs[h % 2], bCONST], writes=[bPB[3]])
                S.group("vector", [lambda E, h=h: E.tensor_copy(
                    rows[0:1, h * 256:(h + 1) * 256], PBIG[0:1, 1792:2048])],
                    reads=[bPB[3]], writes=[bROWS])

            for h in range(4):
                po, pob = POt[h % 2], bPO[h % 2]
                fns = []
                for bi, (sb, tb) in enumerate([(0, 0), (0, 1), (1, 1)]):
                    for kc2 in range(2):
                        c = 2 * h + kc2
                        fns.append(lambda E, c=c, sb=sb, tb=tb, bi=bi, kc2=kc2:
                                   E.matmul(PBIG[:, bi * 128:(bi + 1) * 128],
                                            B3t[:, c * 256 + sb * 128:c * 256 + sb * 128 + 128],
                                            B2t[:, c * 256 + tb * 128:c * 256 + tb * 128 + 128],
                                            start=(kc2 == 0), stop=(kc2 == 1)))
                S.group("tensor", fns, reads=[bB2, bB3], writes=[bPB[0]])
                fns = [
                    lambda E: E.tensor_tensor(PMt[:, 0:128], PBIG[:, 0:128],
                                              maskut[:, :], ALU.mult),
                    lambda E: E.tensor_copy(PMt[:, 128:256], PBIG[:, 128:256]),
                    lambda E: E.tensor_tensor(PMt[:, 256:384], PBIG[:, 256:384],
                                              maskut[:, :], ALU.mult),
                ]
                S.group("vector", fns, reads=[bPB[0], bCONST], writes=[bPMm])
                for kc2 in range(2):
                    bank = 1 + kc2
                    fns = [lambda E, sb=sb, h=h, kc2=kc2, bank=bank: E.matmul(
                        PBIG[:, bank * 512:(bank + 1) * 512],
                        B5t[:, sb * 1024 + (2 * h + kc2) * 128:sb * 1024 + (2 * h + kc2) * 128 + 128],
                        V1t[:, sb * 2048 + h * 512:sb * 2048 + h * 512 + 512],
                        start=(sb == 0), stop=(sb == 1)) for sb in range(2)]
                    S.group("tensor", fns, reads=[bB5] + bV1q, writes=[bPB[bank]])
                    S.group("vector", [lambda E, h=h, kc2=kc2, bank=bank:
                                       E.scalar_tensor_tensor(
                        Ss[:, (2 * h + kc2) * 512:(2 * h + kc2 + 1) * 512],
                        Ss[:, (2 * h + kc2) * 512:(2 * h + kc2 + 1) * 512],
                        eac[:, 2 * h + kc2:2 * h + kc2 + 1],
                        PBIG[:, bank * 512:(bank + 1) * 512],
                        ALU.mult, ALU.add)],
                        reads=[bPB[bank], bEAC], writes=[bS])
                fns = []
                slot = (h % 2) * 1024
                for m4 in range(4):
                    for kc2 in range(2):
                        fns.append(lambda E, h=h, m4=m4, kc2=kc2, p=po, sl=slot: E.matmul(
                            p[:, m4 * 256:(m4 + 1) * 256],
                            B6t[:, sl + kc2 * 512 + m4 * 128:sl + kc2 * 512 + m4 * 128 + 128],
                            B2t[:, (2 * h + kc2) * 256:(2 * h + kc2 + 1) * 256],
                            start=(kc2 == 0), stop=False))
                    for bi, (tb, sb) in enumerate([(0, 0), (1, 0), (1, 1)]):
                        fns.append(lambda E, h=h, m4=m4, tb=tb, sb=sb, bi=bi, p=po:
                                   E.matmul(
                            p[:, m4 * 256 + tb * 128:m4 * 256 + tb * 128 + 128],
                            V1t[:, sb * 2048 + h * 512 + m4 * 128:sb * 2048 + h * 512 + m4 * 128 + 128],
                            PMt[:, bi * 128:(bi + 1) * 128],
                            start=False, stop=(bi == 2)))
                S.group("tensor", fns, reads=[bSBF[h % 2], bB2] + bV1q + [bPMm], writes=[pob])
                if h + 2 < 4:
                    S.group("vector", [lambda E, h2=h + 2: E.tensor_copy(
                        B6t[:, (h2 % 2) * 1024:(h2 % 2) * 1024 + 1024],
                        Ss[:, 2 * h2 * 512:(2 * h2 + 2) * 512])],
                        reads=[bS], writes=[bSBF[h % 2]])
                S.group("scalar", [lambda E, p=po, sl=slot: E.activation(
                    SQ1[:, sl:sl + 1024], p[:, :], AF.Square)],
                    reads=[pob], writes=[bSQs[h % 2]])
                S.group("scalar", [lambda E, h=h, p=po: E.copy(
                    ot[:, h * 1024:(h + 1) * 1024], p[:, :])],
                    reads=[pob], writes=[ob])
                if h > 0:
                    scan_tail(h - 1)
            scan_tail(3)
            fns = [lambda E, c=c: E.matmul(
                PBIG[:, 1536 + c:1536 + c + 1],
                rows[0:1, (c // 2) * 256 + (c % 2) * 128:
                     (c // 2) * 256 + (c % 2) * 128 + 128],
                onesr[0:1, 0:1], start=True, stop=True) for c in range(8)]
            S.group("tensor", fns, reads=[bROWS, bCONST], writes=[bPB[3]])
            S.group("vector", [lambda E, i=i: E.tensor_copy(
                rowsA[:, i * 8:(i + 1) * 8], PBIG[:, 1536:1544])],
                reads=[bPB[3]], writes=[bRA])
            S.group("sync", [lambda E, t=t0, o=ot: E.dma_start(
                out=orms_d[:, :, t:t + T],
                in_=o[:, :].rearrange("p (ch t) -> p ch t", ch=16))],
                reads=[ob], dma=True)

        a2_obh = [[], []]

        def arena_load_r(off, srcb, kc, ncols_per_kc, rbuf, wbuf):
            dst = arena[:, off:off + kc * ncols_per_kc].rearrange(
                "p (kc f) -> p kc f", kc=kc)
            src_ap = srcb[:, :].rearrange("(kc p) f -> p kc f", p=128)
            S.group("sync", [lambda E, d=dst, s_=src_ap: E.dma_start(out=d, in_=s_)],
                    reads=[rbuf], writes=[wbuf], dma=True)

        def load_phase_a2_weights():
            arena_load_r(A2_WG, wgbf_d, 8, 2048, bWBF["wg"], bARL)
            arena_load_r(A2_WO, wobf_d, 16, 1024, bWBF["wo"], bARH)

        def load_a2(i):
            t0 = i * T
            S.group("sync", [lambda E, d=Xt[i % 2], t=t0: E.dma_start(
                out=d[:, :].rearrange("p (tc d) -> p tc d", tc=2),
                in_=x_d[t:t + T, :].rearrange("(tc p) d -> p tc d", p=128))],
                writes=[bX[i % 2]], dma=True)
            S.group("gpsimd", [lambda E, d=[B2t, B1t][i % 2], t=t0: E.dma_start(
                out=d[:, :].rearrange("p (kc t) -> p kc t", kc=8),
                in_=xt_d[:, t:t + T].rearrange("(kc p) t -> p kc t", p=128))],
                writes=[[bB2, bB1][i % 2]], dma=True)
            S.group("sync", [lambda E, d=Ot[i % 2], t=t0: E.dma_start(
                out=d[:, :].rearrange("p (ch t) -> p ch t", ch=16),
                in_=orms_d[:, :, t:t + T])],
                writes=[bO[i % 2]] + a2_obh[i % 2], dma=True)

        def emit_rstdA():
            xcol, fbc, ya_, tt_, yb_ = (F1t[:, 0:128], F1t[:, 128:256],
                                        F1t[:, 256:384], F1t[:, 384:512],
                                        F1t[:, 512:640])
            for fn in [
                lambda E: E.tensor_scalar(xcol, rowsA[:, :], float(1.0 / 512.0),
                                          float(EPS), ALU.mult, ALU.add),
                lambda E: E.tensor_copy(fbc, xcol.bitcast(I32)),
                lambda E: E.tensor_scalar(tt_, fbc, -0.5, MAGICF, ALU.mult, ALU.add),
                lambda E: E.tensor_copy(ya_.bitcast(I32), tt_),
            ]:
                S.group("vector", [fn], reads=[bRA], writes=[bF1])
            cur, nxt = ya_, yb_
            for it in range(3):
                for fn in [
                    lambda E, c=cur: E.tensor_tensor(tt_, c, c, ALU.mult),
                    lambda E: E.tensor_tensor(tt_, tt_, xcol, ALU.mult),
                    lambda E: E.tensor_scalar(tt_, tt_, -0.5, 1.5, ALU.mult, ALU.add),
                    lambda E, c=cur, n=nxt: E.tensor_tensor(n, c, tt_, ALU.mult),
                ]:
                    S.group("vector", [fn], writes=[bF1])
                cur, nxt = nxt, cur
            S.group("vector", [lambda E, c=cur: E.tensor_copy(rstdA[:, :], c)],
                    reads=[bF1], writes=[bRS])

        def emit_a2_tile(i):
            t0 = i * T
            xb, xt = bX[i % 2], Xt[i % 2]
            ob, ot = bO[i % 2], Ot[i % 2]
            if i + 1 < NT:
                load_a2(i + 1)
            xfm = [B2t, B1t][i % 2]
            bXF = [bB2, bB1][i % 2]
            obh = [Buf(f"o2h{i}_0"), Buf(f"o2h{i}_1")]
            for qtr in range(4):
                b0 = (qtr % 2) * 2
                fns = []
                for mm in range(4):
                    m = qtr * 4 + mm
                    for kc in range(8):
                        fns.append(lambda E, m=m, mm=mm, kc=kc, b0=b0, xfm=xfm: E.matmul(
                            PBIG[:, b0 * 512 + mm * 256:b0 * 512 + (mm + 1) * 256],
                            arena[:, A2_WG + kc * 2048 + m * 128:A2_WG + kc * 2048 + m * 128 + 128],
                            xfm[:, kc * 256:(kc + 1) * 256],
                            start=(kc == 0), stop=(kc == 7)))
                S.group("tensor", fns, reads=[bARL, bXF],
                        writes=[bPB[b0], bPB[b0 + 1]])
                S.group("scalar", [lambda E, qtr=qtr, b0=b0: E.activation(
                    F1t[:, (qtr % 2) * 1024:(qtr % 2) * 1024 + 1024],
                    PBIG[:, b0 * 512:(b0 + 2) * 512], AF.Silu)],
                    reads=[bPB[b0], bPB[b0 + 1]], writes=[bF1])
                fns = [lambda E, mm=mm, qtr=qtr: E.tensor_scalar(
                    V1t[:, qtr * 1024 + mm * 256:qtr * 1024 + (mm + 1) * 256],
                    F1t[:, (qtr % 2) * 1024 + mm * 256:(qtr % 2) * 1024 + (mm + 1) * 256],
                    gnw4[:, (qtr * 4 + mm) % 4:(qtr * 4 + mm) % 4 + 1], None,
                    ALU.mult) for mm in range(4)]
                S.group("vector", fns, reads=[bF1, bCONST], writes=[bV1q[qtr]])
                if qtr % 2 == 1:
                    kh = qtr // 2
                    S.group("vector", [lambda E, o=ot, kh=kh: E.tensor_tensor(
                        o[:, kh * 2048:(kh + 1) * 2048],
                        o[:, kh * 2048:(kh + 1) * 2048],
                        V1t[:, kh * 2048:(kh + 1) * 2048], ALU.mult)],
                        reads=[bV1q[2 * kh], bV1q[2 * kh + 1], ob], writes=[obh[kh]])
            for tb in range(2):
                for h in range(4):
                    j = tb * 4 + h
                    po, pob = POt[j % 2], bPO[j % 2]
                    fns = []
                    for ng in range(2):
                        for kk in range(4):
                            kc = 4 * h + kk
                            fns.append(lambda E, tb=tb, ng=ng, kc=kc, kk=kk, p=po: E.matmul(
                                p[:, ng * 512:(ng + 1) * 512],
                                ot[:, kc * 256 + tb * 128:kc * 256 + tb * 128 + 128],
                                arena[:, A2_WO + kc * 1024 + ng * 512:A2_WO + kc * 1024 + ng * 512 + 512],
                                start=(kk == 0), stop=(kk == 3)))
                    S.group("tensor", fns, reads=[bARH, obh[h // 2]], writes=[pob])
                    rc = rstdA[:, i * 8 + h * 2 + tb:i * 8 + h * 2 + tb + 1]
                    if h == 0:
                        S.group("vector", [lambda E, tb=tb, p=po, rc=rc: E.tensor_scalar(
                            F1t[:, tb * 1024:(tb + 1) * 1024], p[:, :], rc, None,
                            ALU.mult)], reads=[pob, bRS], writes=[bF1])
                    else:
                        S.group("vector", [lambda E, tb=tb, p=po, rc=rc:
                                           E.scalar_tensor_tensor(
                            F1t[:, tb * 1024:(tb + 1) * 1024], p[:, :], rc,
                            F1t[:, tb * 1024:(tb + 1) * 1024], ALU.mult, ALU.add)],
                            reads=[pob, bRS], writes=[bF1])
            a2_obh[i % 2] = obh
            for tb in range(2):
                S.group("vector", [lambda E, tb=tb, x=xt: E.scalar_tensor_tensor(
                    F1t[:, tb * 1024:(tb + 1) * 1024],
                    F1t[:, tb * 1024:(tb + 1) * 1024], 1.0,
                    x[:, tb * 1024:(tb + 1) * 1024], ALU.mult, ALU.add,
                    accum_out=statr[:, tb:tb + 1])],
                    reads=[xb], writes=[bF1, bSTAT])
                S.group("vector", [lambda E, tb=tb: E.scalar_tensor_tensor(
                    F2t[:, tb * 1024:(tb + 1) * 1024],
                    F1t[:, tb * 1024:(tb + 1) * 1024], 1.0,
                    F1t[:, tb * 1024:(tb + 1) * 1024], ALU.mult, ALU.mult,
                    accum_out=statr[:, 2 + tb:3 + tb])],
                    reads=[bF1], writes=[bF2, bSTAT])
            emit_ln_tail(i, bF1, F1t, bY[i % 2], Yt[i % 2])
            S.group("sync", [lambda E, t=t0, y=Yt[i % 2]: E.dma_start(
                out=ybuf_d[t:t + T, :].rearrange("(tc p) d -> p tc d", p=128),
                in_=y[:, :].rearrange("p (tc d) -> p tc d", tc=2))],
                reads=[bY[i % 2]], dma=True)

        def emit_ln_tail(i, srcb, srct, dstb, dstt):
            inv = float(1.0 / D)
            S.group("vector", [lambda E: E.tensor_scalar(
                statr[:, 4:6], statr[:, 0:2], inv, None, ALU.mult)], writes=[bSTAT])
            S.group("vector", [lambda E: E.tensor_scalar(
                statr[:, 6:8], statr[:, 2:4], inv, None, ALU.mult)], writes=[bSTAT])
            S.group("vector", [lambda E: E.tensor_tensor(
                statr[:, 8:10], statr[:, 4:6], statr[:, 4:6], ALU.mult)], writes=[bSTAT])
            S.group("vector", [lambda E: E.tensor_tensor(
                statr[:, 10:12], statr[:, 6:8], statr[:, 8:10], ALU.subtract)], writes=[bSTAT])
            S.group("vector", [lambda E: E.tensor_scalar(
                statr[:, 24:26], statr[:, 4:6], -1.0, None, ALU.mult)], writes=[bSTAT])
            x, fb, y0, ya, tt, yb = (statr[:, 12:14], statr[:, 14:16],
                                     statr[:, 16:18], statr[:, 18:20],
                                     statr[:, 20:22], statr[:, 22:24])
            for fn in [
                lambda E: E.tensor_scalar(x, statr[:, 10:12], float(EPS), None, ALU.add),
                lambda E: E.tensor_copy(fb, x.bitcast(I32)),
                lambda E: E.tensor_scalar(y0, fb, -0.5, MAGICF, ALU.mult, ALU.add),
                lambda E: E.tensor_copy(ya.bitcast(I32), y0),
            ]:
                S.group("vector", [fn], writes=[bSTAT])
            cur = ya
            for it in range(3):
                nxt = yb if cur is ya else ya
                for fn in [
                    lambda E, c=cur: E.tensor_tensor(tt, c, c, ALU.mult),
                    lambda E: E.tensor_tensor(tt, tt, x, ALU.mult),
                    lambda E: E.tensor_scalar(tt, tt, -0.5, 1.5, ALU.mult, ALU.add),
                    lambda E, c=cur, n=nxt: E.tensor_tensor(n, c, tt, ALU.mult),
                ]:
                    S.group("vector", [fn], writes=[bSTAT])
                cur = nxt
            rcol = statr[:, 18:20] if cur is ya else statr[:, 22:24]
            fns = [lambda E, tb=tb, rc=rcol: E.tensor_scalar(
                dstt[:, tb * 1024:(tb + 1) * 1024], srct[:, tb * 1024:(tb + 1) * 1024],
                statr[:, 24 + tb:25 + tb], rc[:, tb:tb + 1],
                ALU.add, ALU.mult) for tb in range(2)]
            S.group("vector", fns, reads=[srcb, bSTAT], writes=[dstb])

        def load_phase_b_weights():
            arena_load_r(B_WR, wrtbf_d, 8, 1024, bWBF["wrt"], bARL)
            arena_load_r(B_WK, wktbf_d, 8, 1024, bWBF["wkt"], bARL)
            arena_load_r(B_WV, wvtbf_d, 8, 1024, bWBF["wvt"], bARH)

        def load_b(i):
            t0 = i * T
            S.group("gpsimd", [lambda E, d=Yt[i % 2], t=t0: E.dma_start(
                out=d[:, :].rearrange("p (tc d) -> p tc d", tc=2),
                in_=ybuf_d[t:t + T, :].rearrange("(tc p) d -> p tc d", p=128))],
                writes=[bY[i % 2]], dma=True)

        def b_front(i):
            yb, yt = bY[i % 2], Yt[i % 2]
            xrb, xkb = [bB4, bB4u][i % 2], [bB5, bB5u][i % 2]
            xrt, xkt = [B4t, B4u][i % 2], [B5t, B5u][i % 2]
            S.group("scalar", [lambda E, s=yt: E.copy(B1t[:, :], s[:, :])],
                    reads=[yb], writes=[bB1])
            for r in range(4):
                bank = r % 2
                fns = []
                for jj in range(4):
                    j = r * 4 + jj
                    tc, kc = j // 8, j % 8
                    so = tc * 1024 + kc * 128
                    fns.append(lambda E, so=so, bank=bank, o=jj * 128:
                               E.matmul(PBIG[:, bank * 512 + o:bank * 512 + o + 128],
                                        B1t[:, so:so + 128],
                                        ident[:, :], start=True, stop=True))
                S.group("tensor", fns, reads=[bCONST, bB1], writes=[bPB[bank]])
                tcr, kc0 = (r * 4) // 8, (r * 4) % 8
                out_ap = B2t[:, :].rearrange("p (kc t) -> p kc t", kc=8)[
                    :, kc0:kc0 + 4, tcr * 128:(tcr + 1) * 128]
                S.group("scalar", [lambda E, o=out_ap, bank=bank: E.copy(
                    o, PBIG[:, bank * 512:bank * 512 + 512].rearrange(
                        "p (b t) -> p b t", b=4))],
                    reads=[bPB[bank]], writes=[bB2])
            pc_in, pc_out = bPCOL[i % 2], bPCOL[(i + 1) % 2]
            fns = [lambda E, kc=kc: E.tensor_copy(
                B3t[:, kc * 256 + 1:(kc + 1) * 256],
                B2t[:, kc * 256:(kc + 1) * 256 - 1]) for kc in range(8)]
            fns.append(lambda E, p=PCt[i % 2]: E.tensor_copy(B3t[:, 0::256], p[:, :]))
            S.group("vector", fns, reads=[bB2, pc_in], writes=[bB3])
            S.group("vector", [lambda E, p=PCt[(i + 1) % 2]: E.tensor_copy(
                p[:, :], B2t[:, 255::256])], reads=[bB2], writes=[pc_out])
            S.group("vector", [lambda E: E.tensor_tensor(B6t[:, :], B2t[:, :],
                                                         B3t[:, :], ALU.subtract)],
                    reads=[bB2, bB3], writes=[bB6])
            fns = [lambda E, kc=kc, o=xrt: E.scalar_tensor_tensor(
                o[:, kc * 256:(kc + 1) * 256], B6t[:, kc * 256:(kc + 1) * 256],
                murc[:, kc:kc + 1], B3t[:, kc * 256:(kc + 1) * 256],
                ALU.mult, ALU.add) for kc in range(8)]
            S.group("vector", fns, reads=[bB6, bB3, bCONST], writes=[xrb])
            fns = [lambda E, kc=kc, o=xkt: E.scalar_tensor_tensor(
                o[:, kc * 256:(kc + 1) * 256], B6t[:, kc * 256:(kc + 1) * 256],
                mukc[:, kc:kc + 1], B3t[:, kc * 256:(kc + 1) * 256],
                ALU.mult, ALU.add) for kc in range(8)]
            S.group("vector", fns, reads=[bB6, bB3, bCONST], writes=[xkb])

        def emit_b_tile(i):
            t0 = i * T
            yb, yt = bY[i % 2], Yt[i % 2]
            xrb, xkb = [bB4, bB4u][i % 2], [bB5, bB5u][i % 2]
            xrt, xkt = [B4t, B4u][i % 2], [B5t, B5u][i % 2]
            kab = [bB4, bB4u][i % 2]
            kat = [B4t, B4u][i % 2]
            if i + 1 < NT:
                load_b(i + 1)
            for tb in range(2):
                po, pob = POt[tb], bPO[tb]
                fns = []
                for ng in range(2):
                    for kc in range(8):
                        fns.append(lambda E, tb=tb, ng=ng, kc=kc, p=po, x=xrt: E.matmul(
                            p[:, ng * 512:(ng + 1) * 512],
                            x[:, kc * 256 + tb * 128:kc * 256 + tb * 128 + 128],
                            arena[:, B_WR + kc * 1024 + ng * 512:B_WR + kc * 1024 + ng * 512 + 512],
                            start=(kc == 0), stop=(kc == 7)))
                S.group("tensor", fns, reads=[bARL, xrb], writes=[pob])
                S.group("scalar", [lambda E, tb=tb, p=po: E.activation(
                    F2t[:, tb * 1024:(tb + 1) * 1024], p[:, :], AF.Sigmoid)],
                    reads=[pob], writes=[bF2])
            fns = []
            for m in range(8):
                for kc in range(8):
                    fns.append(lambda E, m=m, kc=kc, x=xkt: E.matmul(
                        PBIG[:, m * 256:(m + 1) * 256],
                        arena[:, B_WK + kc * 1024 + m * 128:B_WK + kc * 1024 + m * 128 + 128],
                        x[:, kc * 256:(kc + 1) * 256],
                        start=(kc == 0), stop=(kc == 7)))
            S.group("tensor", fns, reads=[bARL, xkb], writes=bPB)
            S.group("vector", [lambda E: E.tensor_scalar(B6t[:, :], PBIG[:, :],
                                                         0.0, None, ALU.max)],
                    reads=list(bPB), writes=[bB6])
            S.group("vector", [lambda E, o=kat: E.tensor_tensor(o[:, :], B6t[:, :],
                                                                B6t[:, :], ALU.mult)],
                    reads=[bB6], writes=[kab])
            if i + 1 < NT:
                b_front(i + 1)
            for tb in range(2):
                po, pob = POt[tb], bPO[tb]
                fns = []
                for ng in range(2):
                    for kc in range(8):
                        fns.append(lambda E, tb=tb, ng=ng, kc=kc, p=po, x=kat: E.matmul(
                            p[:, ng * 512:(ng + 1) * 512],
                            x[:, kc * 256 + tb * 128:kc * 256 + tb * 128 + 128],
                            arena[:, B_WV + kc * 1024 + ng * 512:B_WV + kc * 1024 + ng * 512 + 512],
                            start=(kc == 0), stop=(kc == 7)))
                S.group("tensor", fns, reads=[bARH, kab], writes=[pob])
                S.group("vector", [lambda E, tb=tb, p=po: E.tensor_tensor(
                    F1t[:, tb * 1024:(tb + 1) * 1024], p[:, :],
                    F2t[:, tb * 1024:(tb + 1) * 1024], ALU.mult)],
                    reads=[pob, bF2], writes=[bF1])
                S.group("vector", [lambda E, tb=tb, y=yt: E.scalar_tensor_tensor(
                    F1t[:, tb * 1024:(tb + 1) * 1024],
                    F1t[:, tb * 1024:(tb + 1) * 1024], 1.0,
                    y[:, tb * 1024:(tb + 1) * 1024], ALU.mult, ALU.add,
                    accum_out=statr[:, tb:tb + 1])],
                    reads=[yb], writes=[bF1, bSTAT])
                S.group("vector", [lambda E, tb=tb: E.scalar_tensor_tensor(
                    F2t[:, tb * 1024:(tb + 1) * 1024],
                    F1t[:, tb * 1024:(tb + 1) * 1024], 1.0,
                    F1t[:, tb * 1024:(tb + 1) * 1024], ALU.mult, ALU.mult,
                    accum_out=statr[:, 2 + tb:3 + tb])],
                    reads=[bF1], writes=[bF2, bSTAT])
            emit_ln_tail(i, bF1, F1t, bX[i % 2], Xt[i % 2])
            S.group("sync", [lambda E, t=t0, o=Xt[i % 2]: E.dma_start(
                out=out_d[t:t + T, :].rearrange("(tc p) d -> p tc d", p=128),
                in_=o[:, :].rearrange("p (tc d) -> p tc d", tc=2))],
                reads=[bX[i % 2]], dma=True)

        if "A1" in phases:
            load_phase_a1_weights()
            load_a1_x(0)
            for i in range(NT):
                emit_a1_tile(i)
        if "A2" in phases:
            if "A1" not in phases:
                for j in range(len(wconv_list)):
                    emit_wconv(j)
            load_a2(0)
            load_phase_a2_weights()
            emit_rstdA()
            for i in range(NT):
                emit_a2_tile(i)
        if "B" in phases:
            for j in range(len(wconv_list)):
                if wconv_list[j][0] not in bWBF:
                    emit_wconv(j)
            load_b(0)
            load_phase_b_weights()
            b_front(0)
            for i in range(NT):
                emit_b_tile(i)

        S.raw_wait("gpsimd", "sync", S.cnt["sync"])
        if S.cnt["gpsimd"]:
            S.q["gpsimd"].append(
                lambda E, v=S.cnt["gpsimd"]: E.wait_ge(S.sem["gpsimd"], v))

        def runner(lst):
            def f(E):
                for fn in lst:
                    fn(E)
            return f

        block.tensor(runner(S.q["tensor"]))
        block.vector(runner(S.q["vector"]))
        block.scalar(runner(S.q["scalar"]))
        block.gpsimd(runner(S.q["gpsimd"]))
        block.sync(runner(S.q["sync"]))

    return nc


def host_inputs(inputs, core):
    f32 = np.float32
    x = np.ascontiguousarray(inputs["L0_attn_x"][core])
    d = {
        "x": x,
        "xt": np.ascontiguousarray(x.T),
        "wq": inputs["L0_attn_Wq"],
        "wk": inputs["L0_attn_Wk"],
        "wv": inputs["L0_attn_Wv"],
        "wg": inputs["L0_attn_Wg"],
        "gw1": inputs["L0_attn_gk_w1"],
        "gw2": inputs["L0_attn_gk_w2"],
        "wo": inputs["L0_attn_Wo"],
        "wrt": np.ascontiguousarray(inputs["L0_ffn_W_r"].T),
        "wkt": np.ascontiguousarray(inputs["L0_ffn_W_k"].T),
        "wvt": np.ascontiguousarray(inputs["L0_ffn_W_v"].T),
        "ident": np.eye(128, dtype=f32),
        "maskut": np.triu(np.ones((128, 128), dtype=f32)),
        "onesb": np.ones((128, 8), dtype=f32),
        "onesr": np.ones((1, 128), dtype=f32),
        "b2c": np.ascontiguousarray(inputs["L0_attn_gk_b2"].reshape(8, 128).T),
        "gnw4": np.ascontiguousarray(inputs["L0_attn_g_norm_w"].reshape(4, 128).T),
        "murc": np.ascontiguousarray(inputs["L0_ffn_mu_r"].reshape(8, 128).T),
        "mukc": np.ascontiguousarray(inputs["L0_ffn_mu_k"].reshape(8, 128).T),
    }
    return d


def kernel(**inputs):
    nc = build_program(paranoid=True)
    in_maps = [host_inputs(inputs, c) for c in range(8)]
    res = run_bass_kernel_spmd(nc, in_maps, core_ids=list(range(8)))
    out = np.stack([res.results[c]["out"] for c in range(8)], axis=0)
    return out.astype(np.float32)
